# version 06
# baseline (speedup 1.0000x reference)
import contextlib
import numpy as np
import concourse.bass as bass
import concourse.mybir as mybir
from concourse.bass_utils import run_bass_kernel_spmd

F32 = mybir.dt.float32
BF16 = mybir.dt.bfloat16
U32 = mybir.dt.uint32
AF = mybir.ActivationFunctionType
ALU = mybir.AluOpType
AX = mybir.AxisListType

D = 1024
NE = 16
NXP = 17
INTER = 512
CH = 128
GRP = 1024
QT = 512
KT = 1024
EPS = 1e-5
NEG = -1e9
RSCALE = 2.5
ISQ = 1.0 / 32.0

ENGS = ("sync", "gpsimd", "vector", "scalar", "tensor")


class Sem:
    __slots__ = ("h", "n")

    def __init__(self, h):
        self.h = h
        self.n = 0


class Prog:

    def __init__(self):
        self.items = []
        self.pending = {}
        self.dmaq = {}

    def emit(self, eng, fn, waits=(), inc=None, inc_n=1):
        w = list(self.pending.pop(eng, ()))
        for t in waits:
            if t is None:
                continue
            s, v = t
            if v > 0:
                assert v <= s.n, "wait on future token"
                w.append((s, v))
        self.items.append((eng, fn, tuple(w), (inc, inc_n) if inc is not None else None))
        if inc is not None:
            inc.n += inc_n
            return inc.n
        return None

    def set_dma_sems(self, eng, sems):
        self.dmaq[eng] = [sems, 0]

    def dma(self, eng, fn, waits=()):
        sems, cnt = self.dmaq[eng]
        s = sems[cnt % len(sems)]
        w = list(waits)
        if s.n:
            w.append((s, s.n))
        v = self.emit(eng, fn, w, inc=s, inc_n=16)
        self.dmaq[eng][1] = cnt + 1
        return (s, v)

    def barrier(self, sems):
        vals = [(s, s.n) for s in sems if s.n > 0]
        for eng in ENGS:
            self.pending[eng] = list(self.pending.get(eng, ())) + vals

    def replay(self, block):
        for name in ENGS:
            items = [(fn, w, inc) for (e, fn, w, inc) in self.items if e == name]

            def body(eng, items=items):
                last = {}
                for fn, waits, inc in items:
                    for s, v in waits:
                        if last.get(id(s), 0) < v:
                            eng.wait_ge(s.h, v)
                            last[id(s)] = v
                    ins = fn(eng)
                    if inc is not None:
                        ins.then_inc(inc[0].h, inc[1])

            getattr(block, name)(body)


def build(S, debug=False):
    grp = min(GRP, S)
    assert S % grp == 0
    gch = grp // CH
    ngrp = S // grp
    nqt = S // QT

    nc = bass.Bass("TRN2", target_bir_lowering=False)

    dt = nc.dram_tensor
    x_in = dt("x", [S, D], F32, kind="ExternalInput").ap()
    wqT_d = dt("wqT", [D, D], F32, kind="ExternalInput").ap()
    wkT_d = dt("wkT", [D, D], F32, kind="ExternalInput").ap()
    wvT_d = dt("wvT", [D, D], F32, kind="ExternalInput").ap()
    w13T = dt("w13T", [NE, D, 2 * INTER], F32, kind="ExternalInput").ap()
    w2T = dt("w2T", [NE, INTER, D], F32, kind="ExternalInput").ap()
    w13sT = dt("w13sT", [D, 2 * INTER], F32, kind="ExternalInput").ap()
    w2sT = dt("w2sT", [INTER, D], F32, kind="ExternalInput").ap()
    gwT = dt("gwT", [D, NE], F32, kind="ExternalInput").ap()
    gbb = dt("gbb", [128, NE], F32, kind="ExternalInput").ap()
    cs_in = dt("cs", [S, D], F32, kind="ExternalInput").ap()
    out_d = dt("out", [S, D], F32, kind="ExternalOutput").ap()

    ik = "ExternalOutput" if debug else "Internal"
    dbg = None
    if debug:
        dbg = {
            "u1f": dt("dbg_u1f", [128, 8, min(GRP, S)], BF16, kind="ExternalOutput").ap(),
            "scg": dt("dbg_scg", [128, min(GRP, S) // CH, NE], F32, kind="ExternalOutput").ap(),
            "comb": dt("dbg_comb", [128, min(GRP, S) // CH, NE], F32, kind="ExternalOutput").ap(),
            "yg": dt("dbg_yg", [128, min(GRP, S) // CH, D], F32, kind="ExternalOutput").ap(),
            "mscg": dt("dbg_mscg", [128, min(GRP, S) // CH, NE], F32, kind="ExternalOutput").ap(),
            "deng": dt("dbg_deng", [128, min(GRP, S) // CH], F32, kind="ExternalOutput").ap(),
            "idxg": dt("dbg_idxg", [128, min(GRP, S) // CH, 8], U32, kind="ExternalOutput").ap(),
            "s0": dt("dbg_s0", [128, INTER], BF16, kind="ExternalOutput").ap(),
            "h0": dt("dbg_h0", [128, INTER], BF16, kind="ExternalOutput").ap(),
            "hf0": dt("dbg_hf0", [128, INTER], BF16, kind="ExternalOutput").ap(),
            "y0": dt("dbg_y0", [128, D], F32, kind="ExternalOutput").ap(),
            "w13b": dt("dbg_w13b", [128, 8, D], BF16, kind="ExternalOutput").ap(),
            "gu0": dt("dbg_gu0", [128, D], F32, kind="ExternalOutput").ap(),
            "ye": dt("dbg_ye", [NXP, 128, D], F32, kind="ExternalOutput").ap(),
        }
    wb13 = dt("wb13", [NXP, D, 2 * INTER], BF16, kind="Internal").ap()
    wb2 = dt("wb2", [NXP, INTER, D], BF16, kind="Internal").ap()
    wbqkv = dt("wbqkv", [3, D, D], BF16, kind="Internal").ap()
    x1d = dt("x1d", [S, D], F32, kind=ik).ap()
    x2d = dt("x2d", [S, D], F32, kind=ik).ap()
    Qf = dt("Qf", [D, S], BF16, kind=ik).ap()
    Kf = dt("Kf", [D, S], BF16, kind=ik).ap()
    Vt_d = dt("Vt", [S, D], BF16, kind=ik).ap()

    p = Prog()
    outer = contextlib.ExitStack()
    with outer:
        E = outer.enter_context
        sPE = Sem(E(nc.semaphore("sPE")))
        sDV = Sem(E(nc.semaphore("sDV")))
        sAC = Sem(E(nc.semaphore("sAC")))
        sGI = Sem(E(nc.semaphore("sGI")))
        dma_sems = {}
        for qname, nq in (("sync", 3), ("gpsimd", 4), ("scalar", 3)):
            dma_sems[qname] = [Sem(E(nc.semaphore(f"sQ_{qname}{i}")))
                               for i in range(nq)]
            p.set_dma_sems(qname, dma_sems[qname])
        ALL_SEMS = [sPE, sDV, sAC, sGI] + [s for v in dma_sems.values()
                                           for s in v]

        ident = E(nc.sbuf_tensor("ident", [128, 128], BF16))
        identF = E(nc.sbuf_tensor("identF", [128, 128], F32))
        iota16 = E(nc.sbuf_tensor("iota16", [128, NE], F32))
        gw_sb = E(nc.sbuf_tensor("gw_sb", [128, 8, NE], F32))
        gbb_sb = E(nc.sbuf_tensor("gbb_sb", [128, NE], F32))
        epsT = E(nc.sbuf_tensor("epsT", [128, 1], F32))
        padT = E(nc.sbuf_tensor("padT", [128, 2], F32))

        def vpad():
            for _ in range(4):
                p.emit("vector", lambda e: e.memset(padT[:, 0:1], 0.0))

        t1 = p.emit("gpsimd", lambda e: e.memset(ident[:], 0.0), inc=sGI)
        t2 = p.emit("gpsimd", lambda e: e.memset(identF[:], 0.0), inc=sGI)
        p.emit("gpsimd", lambda e: e.memset(epsT[:], EPS), inc=sGI)

        p.emit("gpsimd", lambda e: e.affine_select(
            out=ident[:], in_=ident[:], compare_op=ALU.not_equal, fill=1.0,
            base=0, pattern=[[-1, 128]], channel_multiplier=1),
            waits=[(sGI, t1)], inc=sGI)
        p.emit("gpsimd", lambda e: e.affine_select(
            out=identF[:], in_=identF[:], compare_op=ALU.not_equal, fill=1.0,
            base=0, pattern=[[-1, 128]], channel_multiplier=1),
            waits=[(sGI, t2)], inc=sGI)
        p.emit("gpsimd", lambda e: e.iota(
            iota16[:], [[1, NE]], base=0, channel_multiplier=0,
            allow_small_or_imprecise_dtypes=True), inc=sGI)
        tok_const = sGI.n

        tok_gbb = p.dma("gpsimd", lambda e: e.dma_start(out=gbb_sb[:], in_=gbb))

        cast_done = []
        with contextlib.ExitStack() as st:
            stF = [st.enter_context(nc.sbuf_tensor(f"stF{i}", [128, 8, D], F32))
                   for i in range(2)]
            stB = [st.enter_context(nc.sbuf_tensor(f"stB{i}", [128, 8, D], BF16))
                   for i in range(2)]

            tok_gw = p.dma("sync", lambda e: e.dma_start(
                out=gw_sb[:],
                in_=gwT.rearrange("(dc p) e -> p dc e", p=128)))

            jobs = []
            for ee in range(NXP):
                src = (w13T[ee] if ee < NE else w13sT).rearrange(
                    "(dc p) i -> p dc i", p=128)
                jobs.append((src, wb13[ee].rearrange("(dc p) i -> p dc i", p=128), 8))
            for ee in range(NXP):
                src = (w2T[ee] if ee < NE else w2sT).rearrange(
                    "(ic p) d -> p ic d", p=128)
                jobs.append((src, wb2[ee].rearrange("(ic p) d -> p ic d", p=128), 4))
            for j, w in enumerate([wqT_d, wkT_d, wvT_d]):
                jobs.append((w.rearrange("(dc p) e -> p dc e", p=128),
                             wbqkv[j].rearrange("(dc p) e -> p dc e", p=128), 8))

            rd_tok = [0, 0]
            st_tok = [None, None]
            for ji, (src, dst, nf) in enumerate(jobs):
                b = ji % 2
                ld = p.dma("sync", lambda e, src=src, nf=nf, b=b: e.dma_start(
                    out=stF[b][:, 0:nf, :], in_=src),
                    waits=[(sDV, rd_tok[b])])
                rd_tok[b] = p.emit("vector", lambda e, nf=nf, b=b: e.tensor_copy(
                    stB[b][:, 0:nf, :], stF[b][:, 0:nf, :]),
                    waits=[ld, st_tok[b]], inc=sDV)
                st_tok[b] = p.dma("gpsimd", lambda e, dst=dst, nf=nf, b=b: e.dma_start(
                    out=dst, in_=stB[b][:, 0:nf, :]),
                    waits=[(sDV, rd_tok[b])])
                cast_done.append(st_tok[b])
        p.barrier(ALL_SEMS)

        def emit_ffn_pass(src_dram, dst_dram, with_qkv, tag, dbg=None):
            with contextlib.ExitStack() as st:
                A = st.enter_context
                xg = A(nc.sbuf_tensor(tag + "xg", [128, 2 * gch, D], F32))
                u1f = A(nc.sbuf_tensor(tag + "u1f", [128, 16, grp], BF16))
                yg = A(nc.sbuf_tensor(tag + "yg", [128, gch, D], F32))
                wbuf = A(nc.sbuf_tensor(tag + "wbuf", [128, 24576], BF16))
                s_bf = A(nc.sbuf_tensor(tag + "s_bf", [128, 2, INTER], BF16))
                h_bf = A(nc.sbuf_tensor(tag + "h_bf", [128, INTER], BF16))
                hf_bf = A(nc.sbuf_tensor(tag + "hf_bf", [128, INTER], BF16))
                comb = A(nc.sbuf_tensor(tag + "comb", [128, 2 * gch, NE], F32))
                rsg = A(nc.sbuf_tensor(tag + "rsg", [128, gch], F32))
                rqg = A(nc.sbuf_tensor(tag + "rqg", [128, gch], F32))
                mvm = A(nc.sbuf_tensor(tag + "mvm", [128, gch], F32))
                mvq = A(nc.sbuf_tensor(tag + "mvq", [128, gch], F32))
                veg = A(nc.sbuf_tensor(tag + "veg", [128, gch], F32))
                jnk = A(nc.sbuf_tensor(tag + "jnk", [128, D], F32))
                u1g32 = A(nc.sbuf_tensor(tag + "u1g32", [128, D], F32))
                lgg = A(nc.sbuf_tensor(tag + "lgg", [128, gch, NE], F32))
                scg = A(nc.sbuf_tensor(tag + "scg", [128, gch, NE], F32))
                big = A(nc.sbuf_tensor(tag + "big", [128, NE], F32))
                m8t = A(nc.sbuf_tensor(tag + "m8t", [128, 8], F32))
                idx8g = A(nc.sbuf_tensor(tag + "idx8g", [128, gch, 8], U32))
                idx8f = A(nc.sbuf_tensor(tag + "idx8f", [128, 8], F32))
                mscg = A(nc.sbuf_tensor(tag + "mscg", [128, gch, NE], F32))
                mskt = A(nc.sbuf_tensor(tag + "mskt", [128, NE], F32))
                tmp16 = A(nc.sbuf_tensor(tag + "tmp16", [128, NE], F32))
                msk2 = A(nc.sbuf_tensor(tag + "msk2", [128, NE], F32))
                msk3 = A(nc.sbuf_tensor(tag + "msk3", [128, NE], F32))
                deng = A(nc.sbuf_tensor(tag + "deng", [128, gch], F32))
                if with_qkv:
                    cs_sb = [A(nc.sbuf_tensor(tag + "cs_sb0", [128, D], F32))]
                    cs_sb = cs_sb + cs_sb
                    rt1 = jnk[:, 0:INTER]
                    rt2 = jnk[:, INTER:2 * INTER]
                    rot = A(nc.sbuf_tensor(tag + "rot", [128, D], BF16))
                    fstage = A(nc.sbuf_tensor(tag + "fstage", [128, D], BF16))
                    vstage = A(nc.sbuf_tensor(tag + "vstage", [128, D], BF16))

                guP = [A(nc.psum_tensor(tag + "guA", [128, D], F32)),
                       A(nc.psum_tensor(tag + "guB", [128, D], F32))]
                zzP = A(nc.psum_tensor(tag + "zz", [128, D], F32))
                htP = A(nc.psum_tensor(tag + "ht", [128, D], BF16))
                spP = A(nc.psum_tensor(tag + "sp", [128, 512], F32))

                def w13buf(par):
                    return wbuf[:, par * 8192:(par + 1) * 8192].rearrange(
                        "p (dc i) -> p dc i", i=D)

                def w2buf(par):
                    return wbuf[:, 16384 + par * 4096:16384 + (par + 1) * 4096] \
                        .rearrange("p (ic d) -> p ic d", d=D)

                wq_b = wbuf[:, 0:8192].rearrange("p (dc e) -> p dc e", e=D)
                wk_b = wbuf[:, 8192:16384].rearrange("p (dc e) -> p dc e", e=D)
                wv_b = wbuf[:, 16384:24576].rearrange("p (dc e) -> p dc e", e=D)

                last_read = {"w13_0": 0, "w13_1": 0, "w2_0": 0, "w2_1": 0}
                wtok = {}
                xg_freed = [None, None]
                u1f_freed = [0, 0]
                sp_free = [0]
                ht_free = [0]
                gu_guard = [0, 0]
                sil_guard = [0, 0]
                zz_guard = [0]
                pre_tok = {}

                def load_weights(g, e):
                    par = e % 2
                    t1 = p.dma("sync", lambda eng, e=e, par=par: eng.dma_start(
                        out=w13buf(par),
                        in_=wb13[e].rearrange("(dc p) i -> p dc i", p=128)),
                        waits=[(sPE, last_read[f"w13_{par}"]),
                               cast_done[e]])
                    t2 = p.dma("sync", lambda eng, e=e, par=par: eng.dma_start(
                        out=w2buf(par),
                        in_=wb2[e].rearrange("(ic p) d -> p ic d", p=128)),
                        waits=[(sPE, last_read[f"w2_{par}"]),
                               cast_done[NXP + e]])
                    wtok[(g, e)] = (t1, t2)

                def load_qkv(g):
                    toks = []
                    for j in range(3):
                        dst = [wq_b, wk_b, wv_b][j]
                        regions = [["w13_0"], ["w13_1"], ["w2_0", "w2_1"]][j]
                        t = p.dma("sync", lambda eng, j=j, dst=dst: eng.dma_start(
                            out=dst,
                            in_=wbqkv[j].rearrange("(dc p) e -> p dc e", p=128)),
                            waits=[(sPE, max(last_read[r] for r in regions)),
                                   cast_done[2 * NXP + j]])
                        toks.append(t)
                    wtok[("qkv", g)] = toks

                def emit_stats(xoff, xg_ld):
                    for c in range(gch):
                        p.emit("vector", lambda e, c=c: e.reduce_sum(
                            rsg[:, c:c + 1], xg[:, xoff + c, :], axis=AX.X),
                            waits=[xg_ld])
                    for c in range(gch):
                        p.emit("vector", lambda e, c=c: e.scalar_tensor_tensor(
                            jnk[:], xg[:, xoff + c, :], 1.0, xg[:, xoff + c, :],
                            op0=ALU.mult, op1=ALU.mult,
                            accum_out=rqg[:, c:c + 1]))
                    for _ in range(3):
                        p.emit("vector", lambda e: e.memset(padT[:, 0:1], 0.0))
                    tdv = p.emit("vector", lambda e: e.memset(padT[:, 1:2], 0.0),
                                 inc=sDV)
                    p.emit("scalar", lambda e: e.activation(
                        out=mvm[:], in_=rsg[:], func=AF.Copy, scale=1.0 / D),
                        waits=[(sDV, tdv)])
                    tac = p.emit("scalar", lambda e: e.activation(
                        out=mvq[:], in_=rqg[:], func=AF.Copy, scale=1.0 / D),
                        inc=sAC)
                    p.emit("vector", lambda e: e.tensor_tensor(
                        rsg[:], mvm[:], mvm[:], op=ALU.mult),
                        waits=[(sAC, tac)])
                    vpad()
                    tdv2 = p.emit("vector", lambda e: e.tensor_tensor(
                        veg[:], mvq[:], rsg[:], op=ALU.subtract), inc=sDV)
                    tac2 = p.emit("scalar", lambda e: e.activation(
                        out=veg[:], in_=veg[:], func=AF.Sqrt, bias=epsT[:]),
                        waits=[(sDV, tdv2)], inc=sAC)
                    p.emit("vector", lambda e: e.reciprocal(veg[:], veg[:]),
                           waits=[(sAC, tac2)])
                    vpad()

                def emit_ln_chunk(xoff, uoff, c, gate):
                    tn = p.emit("vector", lambda e, c=c: e.tensor_scalar(
                        jnk[:], xg[:, xoff + c, :], mvm[:, c:c + 1],
                        veg[:, c:c + 1], op0=ALU.subtract, op1=ALU.mult),
                        inc=sDV)
                    ev = 0
                    for rr in range(2):
                        tt = 0
                        for b in range(4):
                            dc = rr * 4 + b
                            tt = p.emit("tensor", lambda e, dc=dc, b=b:
                                e.transpose(
                                    spP[:, b * 128:(b + 1) * 128],
                                    jnk[:, dc * 128:(dc + 1) * 128],
                                    identF[:]),
                                waits=[(sDV, tn), (sDV, sp_free[0]),
                                       (sGI, tok_const)],
                                inc=sPE if b == 3 else None)
                        ev = p.emit("vector", lambda e, c=c, rr=rr:
                            e.tensor_copy(
                                u1f[:, uoff + rr * 4:uoff + (rr + 1) * 4,
                                    c * 128:(c + 1) * 128],
                                spP[:].rearrange("p (b t) -> p b t", t=128)),
                            waits=[(sPE, tt)], inc=sDV)
                        sp_free[0] = ev
                        if gate:
                            ev = p.emit("vector", lambda e, rr=rr:
                                e.tensor_copy(
                                    u1g32[:, rr * 512:(rr + 1) * 512],
                                    spP[:]),
                                inc=sDV)
                            sp_free[0] = ev
                    if gate:
                        tg = 0
                        for dc in range(8):
                            tg = p.emit("tensor", lambda e, dc=dc: e.matmul(
                                spP[:, 0:NE],
                                u1g32[:, dc * 128:(dc + 1) * 128],
                                gw_sb[:, dc, :],
                                start=(dc == 0), stop=(dc == 7)),
                                waits=[(sDV, ev), tok_gw],
                                inc=sPE if dc == 7 else None)
                        ev = p.emit("vector", lambda e, c=c: e.tensor_copy(
                            lgg[:, c, :], spP[:, 0:NE]),
                            waits=[(sPE, tg)], inc=sDV)
                        sp_free[0] = ev
                    return ev

                def emit_routing(coff):
                    tl = sDV.n
                    tsg = p.emit("scalar", lambda e: e.activation(
                        out=scg[:].rearrange("p c e -> p (c e)"),
                        in_=lgg[:].rearrange("p c e -> p (c e)"),
                        func=AF.Sigmoid), waits=[(sDV, tl)], inc=sAC)
                    for c in range(gch):
                        p.emit("vector", lambda e, c=c: e.tensor_tensor(
                            big[:], scg[:, c, :], gbb_sb[:], op=ALU.add),
                            waits=[(sAC, tsg), tok_gbb])
                        vpad()
                        p.emit("vector", lambda e: e.max(m8t[:], big[:]))
                        vpad()
                        p.emit("vector", lambda e, c=c: e.max_index(
                            idx8g[:, c, :], m8t[:], big[:]))
                    for c in range(gch):
                        p.emit("vector", lambda e, c=c: e.tensor_copy(
                            idx8f[:], idx8g[:, c, :]))
                        vpad()
                        for j in range(4):
                            dst = [mskt, tmp16, msk2, msk3][j]
                            p.emit("vector", lambda e, j=j, dst=dst: e.tensor_scalar(
                                dst[:], iota16[:], idx8f[:, j:j + 1], None,
                                op0=ALU.is_equal), waits=[(sGI, tok_const)])
                        vpad()
                        p.emit("vector", lambda e: e.tensor_tensor(
                            mskt[:], mskt[:], tmp16[:], op=ALU.add))
                        p.emit("vector", lambda e: e.tensor_tensor(
                            msk2[:], msk2[:], msk3[:], op=ALU.add))
                        vpad()
                        p.emit("vector", lambda e: e.tensor_tensor(
                            mskt[:], mskt[:], msk2[:], op=ALU.add))
                        vpad()
                        p.emit("vector", lambda e, c=c: e.tensor_tensor(
                            mscg[:, c, :], mskt[:], scg[:, c, :], op=ALU.mult))
                        vpad()
                        p.emit("vector", lambda e, c=c: e.reduce_sum(
                            deng[:, c:c + 1], mscg[:, c, :], axis=AX.X))
                    vpad()
                    vpad()
                    p.emit("vector", lambda e: e.reciprocal(deng[:], deng[:]))
                    vpad()
                    t = 0
                    for c in range(gch):
                        t = p.emit("vector", lambda e, c=c, coff=coff: e.tensor_scalar(
                            comb[:, coff + c, :], mscg[:, c, :],
                            deng[:, c:c + 1], RSCALE,
                            op0=ALU.mult, op1=ALU.mult), inc=sDV)
                    return t

                def prelude(g):
                    par = g % 2
                    xoff = par * gch
                    xg_ld = p.dma("gpsimd", lambda e, g=g, xoff=xoff: e.dma_start(
                        out=xg[:, xoff:xoff + gch, :],
                        in_=src_dram[g * grp:(g + 1) * grp, :].rearrange(
                            "(c p) d -> p c d", p=128)),
                        waits=[xg_freed[par]])
                    yield
                    emit_stats(xoff, xg_ld)
                    yield
                    ev = 0
                    for c in range(gch):
                        if c == 0:
                            sp_free[0] = max(sp_free[0], 0)
                        ev = emit_ln_chunk(xoff, par * 8, c, gate=True)
                        yield
                    ctok = emit_routing(par * gch)
                    pre_tok[g] = {"ev": ev, "comb": ctok, "xg_ld": xg_ld}
                    yield

                def run_prelude_to_end(gen):
                    for _ in gen:
                        pass

                load_weights(0, 0)
                load_weights(0, 1)
                run_prelude_to_end(prelude(0))

                for g in range(ngrp):
                    g0 = g * grp
                    par = g % 2
                    xoff = par * gch
                    ev = pre_tok[g]["ev"]
                    xg_ld = pre_tok[g]["xg_ld"]


                    units = [(e, c) for e in range(NXP) for c in range(gch)]
                    tok_gu = {}
                    tok_tr = {}
                    tok_z = {}
                    tok_h = {-2: gu_guard[0], -1: gu_guard[1]}
                    tok_ev = {-1: ht_free[0]}
                    tok_sil = {-2: sil_guard[0], -1: sil_guard[1]}
                    tok_y = {-1: zz_guard[0]}

                    def pe_trans(u):
                        t = 0
                        for ib in range(4):
                            t = p.emit("tensor", lambda e, ib=ib: e.transpose(
                                htP[:, ib * 128:(ib + 1) * 128],
                                h_bf[:, ib * 128:(ib + 1) * 128], ident[:]),
                                waits=[(sDV, tok_h[u]), (sDV, tok_ev[u - 1])],
                                inc=sPE if ib == 3 else None)
                        tok_tr[u] = t

                    def dv_evac(u):
                        tok_ev[u] = p.emit("vector", lambda e: e.tensor_copy(
                            hf_bf[:], htP[:, 0:512]),
                            waits=[(sPE, tok_tr[u])], inc=sDV)

                    def pe_gu(u):
                        eu, cu = units[u]
                        bpar = u % 2
                        w13t = wtok[(g, eu)][0]
                        t = 0
                        for ih in range(2):
                            for dc in range(8):
                                t = p.emit(
                                    "tensor",
                                    lambda e, cu=cu, dc=dc, ih=ih, bpar=bpar,
                                           eu=eu, xoff=xoff, par=par:
                                    e.matmul(
                                        guP[bpar][:, ih * 512:(ih + 1) * 512],
                                        u1f[:, par * 8 + dc,
                                            cu * 128:(cu + 1) * 128],
                                        w13buf(eu % 2)[:, dc,
                                                       ih * 512:(ih + 1) * 512],
                                        start=(dc == 0), stop=(dc == 7)),
                                    waits=[w13t,
                                           (sDV, tok_h[u - 2]),
                                           (sAC, tok_sil[u - 2]),
                                           (sDV, ev)],
                                    inc=sPE if (ih == 1 and dc == 7) else None)
                        tok_gu[u] = t
                        last_read[f"w13_{eu % 2}"] = t
                        u1f_freed[par] = t

                    def pe_z(u):
                        eu, cu = units[u]
                        w2t = wtok[(g, eu)][1]
                        t = 0
                        for dh in range(2):
                            for ic in range(4):
                                t = p.emit(
                                    "tensor",
                                    lambda e, ic=ic, dh=dh, eu=eu: e.matmul(
                                        zzP[:, dh * 512:(dh + 1) * 512],
                                        hf_bf[:, ic * 128:(ic + 1) * 128],
                                        w2buf(eu % 2)[:, ic,
                                                      dh * 512:(dh + 1) * 512],
                                        start=(ic == 0), stop=(ic == 3)),
                                    waits=[w2t, (sDV, tok_ev[u]),
                                           (sDV, tok_y[u - 1])],
                                    inc=sPE if (dh == 1 and ic == 3) else None)
                        tok_z[u] = t
                        last_read[f"w2_{eu % 2}"] = t

                    def ac_silu(u):
                        bpar = u % 2
                        tok_sil[u] = p.emit("scalar", lambda e, bpar=bpar:
                            e.activation(
                                out=s_bf[:, bpar, :], in_=guP[bpar][:, 0:512],
                                func=AF.Silu),
                            waits=[(sPE, tok_gu[u]), (sDV, tok_h[u - 2])],
                            inc=sAC)

                    def dv_h(u):
                        bpar = u % 2
                        tok_h[u] = p.emit("vector", lambda e, bpar=bpar:
                            e.tensor_tensor(
                                h_bf[:], guP[bpar][:, 512:1024],
                                s_bf[:, bpar, :], op=ALU.mult),
                            waits=[(sAC, tok_sil[u]), (sPE, tok_gu[u])], inc=sDV)

                    def dv_y(u):
                        eu, cu = units[u]
                        coff = par * gch
                        if eu == 0:
                            tok_y[u] = p.emit("vector", lambda e, cu=cu, eu=eu,
                                coff=coff: e.tensor_scalar(
                                    yg[:, cu, :], zzP[:],
                                    comb[:, coff + cu, eu:eu + 1],
                                    None, op0=ALU.mult),
                                waits=[(sPE, tok_z[u])], inc=sDV)
                        elif eu < NE:
                            tok_y[u] = p.emit("vector", lambda e, cu=cu, eu=eu,
                                coff=coff: e.scalar_tensor_tensor(
                                    yg[:, cu, :], zzP[:],
                                    comb[:, coff + cu, eu:eu + 1],
                                    yg[:, cu, :], op0=ALU.mult, op1=ALU.add),
                                waits=[(sPE, tok_z[u])], inc=sDV)
                        else:
                            tok_y[u] = p.emit("vector", lambda e, cu=cu:
                                e.scalar_tensor_tensor(
                                    yg[:, cu, :], zzP[:], 1.0,
                                    yg[:, cu, :], op0=ALU.mult, op1=ALU.add),
                                waits=[(sPE, tok_z[u])], inc=sDV)

                    ngen = prelude(g + 1) if g + 1 < ngrp else None
                    gen_done = ngen is None
                    step = max(1, len(units) // 14)

                    for u in range(len(units)):
                        eu, cu = units[u]
                        if u > 0:
                            pe_trans(u - 1)
                            dv_evac(u - 1)
                        pe_gu(u)
                        if u > 0:
                            pe_z(u - 1)
                        if cu == 0:
                            if eu + 1 < NXP and (g, eu + 1) not in wtok:
                                load_weights(g, eu + 1)
                            elif (eu == NXP - 1 and not with_qkv
                                  and g + 1 < ngrp):
                                load_weights(g + 1, 1)
                        ac_silu(u)
                        dv_h(u)
                        if u > 0:
                            dv_y(u - 1)
                        if (not gen_done and u >= 8 and (u - 8) % step == 0):
                            try:
                                next(ngen)
                            except StopIteration:
                                gen_done = True
                    u_last = len(units) - 1
                    pe_trans(u_last)
                    dv_evac(u_last)
                    pe_z(u_last)
                    dv_y(u_last)
                    if not gen_done:
                        run_prelude_to_end(ngen)
                    if with_qkv:
                        load_qkv(g)
                    elif g + 1 < ngrp:
                        load_weights(g + 1, 0)

                    ep = 0
                    for c in range(gch):
                        ep = p.emit("vector", lambda e, c=c, xoff=xoff:
                            e.scalar_tensor_tensor(
                                xg[:, xoff + c, :], yg[:, c, :], 0.5,
                                xg[:, xoff + c, :],
                                op0=ALU.mult, op1=ALU.add),
                            waits=[(sDV, tok_y[(NXP - 1) * gch + c])], inc=sDV)

                    if not with_qkv:
                        xg_freed[par] = p.dma("gpsimd",
                            lambda e, g0=g0, xoff=xoff: e.dma_start(
                                out=dst_dram[g0:g0 + grp, :].rearrange(
                                    "(c p) d -> p c d", p=128),
                                in_=xg[:, xoff:xoff + gch, :]),
                            waits=[(sDV, ep)])
                        u1f_freed[par] = max(u1f_freed[par], tok_gu[u_last])
                        gu_guard[0] = tok_h[u_last - 1]
                        gu_guard[1] = tok_h[u_last]
                        sil_guard[0] = tok_sil[u_last - 1]
                        sil_guard[1] = tok_sil[u_last]
                        zz_guard[0] = tok_y[u_last]
                        continue

                    emit_stats(xoff, None)
                    ev2 = 0
                    for c in range(gch):
                        ev2 = emit_ln_chunk(xoff, par * 8, c, gate=False)
                    qkvt = wtok[("qkv", g)]
                    ac_g = sAC.n
                    rope_done = [0, 0]
                    psum_free = [0, 0, 0]
                    ht_guard = tok_ev[u_last]
                    fst_store = None
                    vst_store = None
                    for c in range(gch):
                        cpar = 0
                        cld = p.dma("scalar", lambda e, g0=g0, c=c:
                            e.dma_start(
                                out=cs_sb[0][:],
                                in_=cs_in[g0 + c * 128:g0 + (c + 1) * 128, :]),
                            waits=[(sDV, max(rope_done))])
                        for j in range(3):
                            psum = [guP[0], guP[1], zzP][j]
                            wb = [wq_b, wk_b, wv_b][j]
                            wt = qkvt[j]
                            t = 0
                            for eh in range(2):
                                for dc in range(8):
                                    t = p.emit(
                                        "tensor",
                                        lambda e, c=c, dc=dc, eh=eh, psum=psum,
                                               wb=wb, par=par: e.matmul(
                                            psum[:, eh * 512:(eh + 1) * 512],
                                            u1f[:, par * 8 + dc,
                                                c * 128:(c + 1) * 128],
                                            wb[:, dc, eh * 512:(eh + 1) * 512],
                                            start=(dc == 0), stop=(dc == 7)),
                                        waits=[wt, (sDV, ev2),
                                               (sDV, psum_free[j]), (sAC, ac_g)],
                                        inc=sPE if (eh == 1 and dc == 7) else None)
                            u1f_freed[par] = t
                            if j < 2:
                                p.emit("vector", lambda e, psum=psum, cpar=cpar:
                                    e.tensor_tensor(
                                        rt1, psum[:, 0:512],
                                        cs_sb[cpar][:, 0:512], op=ALU.mult),
                                    waits=[(sPE, t), cld])
                                p.emit("vector", lambda e, psum=psum, cpar=cpar:
                                    e.tensor_tensor(
                                        rt2, psum[:, 512:1024],
                                        cs_sb[cpar][:, 512:1024], op=ALU.mult))
                                p.emit("vector", lambda e: e.tensor_tensor(
                                    rot[:, 0:512], rt1, rt2, op=ALU.add))
                                p.emit("vector", lambda e, psum=psum, cpar=cpar:
                                    e.tensor_tensor(
                                        rt1, psum[:, 512:1024],
                                        cs_sb[cpar][:, 0:512], op=ALU.mult))
                                p.emit("vector", lambda e, psum=psum, cpar=cpar:
                                    e.tensor_tensor(
                                        rt2, psum[:, 0:512],
                                        cs_sb[cpar][:, 512:1024], op=ALU.mult))
                                pq = p.emit("vector", lambda e: e.tensor_tensor(
                                    rot[:, 512:1024], rt1, rt2,
                                    op=ALU.subtract), inc=sDV)
                                psum_free[j] = pq
                                rope_done[cpar] = pq
                                tt = 0
                                for dc in range(8):
                                    tt = p.emit("tensor", lambda e, dc=dc:
                                        e.transpose(
                                            htP[:, dc * 128:(dc + 1) * 128],
                                            rot[:, dc * 128:(dc + 1) * 128],
                                            ident[:]),
                                        waits=[(sDV, pq), (sDV, ht_guard)],
                                        inc=sPE if dc == 7 else None)
                                fe = p.emit("vector", lambda e: e.tensor_copy(
                                    fstage[:], htP[:]),
                                    waits=[(sPE, tt), fst_store], inc=sDV)
                                ht_guard = fe
                                dstf = Qf if j == 0 else Kf
                                fst_store = p.dma("gpsimd",
                                    lambda e, dstf=dstf, g0=g0, c=c: e.dma_start(
                                        out=dstf.rearrange(
                                            "(dc p) q -> p dc q", p=128)[
                                            :, :, g0 + c * 128:g0 + (c + 1) * 128],
                                        in_=fstage[:].rearrange(
                                            "p (dc t) -> p dc t", t=128)),
                                    waits=[(sDV, fe)])
                            else:
                                pv = p.emit("vector", lambda e: e.tensor_copy(
                                    vstage[:], zzP[:]),
                                    waits=[(sPE, t), vst_store], inc=sDV)
                                psum_free[2] = pv
                                vst_store = p.dma("gpsimd",
                                    lambda e, g0=g0, c=c: e.dma_start(
                                        out=Vt_d[g0 + c * 128:g0 + (c + 1) * 128, :],
                                        in_=vstage[:]),
                                    waits=[(sDV, pv)])
                    ht_free[0] = ht_guard
                    gu_guard[0] = psum_free[0]
                    gu_guard[1] = psum_free[1]
                    sil_guard[0] = sil_guard[1] = sAC.n
                    zz_guard[0] = psum_free[2]
                    last_read["w13_0"] = sPE.n
                    last_read["w13_1"] = sPE.n
                    last_read["w2_0"] = sPE.n
                    last_read["w2_1"] = sPE.n
                    if g + 1 < ngrp:
                        load_weights(g + 1, 0)
                        load_weights(g + 1, 1)
                    xg_freed[par] = p.dma("gpsimd",
                        lambda e, g0=g0, xoff=xoff: e.dma_start(
                            out=dst_dram[g0:g0 + grp, :].rearrange(
                                "(c p) d -> p c d", p=128),
                            in_=xg[:, xoff:xoff + gch, :]),
                        waits=[(sDV, ep)])
            p.barrier(ALL_SEMS)

        emit_ffn_pass(x_in, x1d, with_qkv=True, tag="a_", dbg=dbg)

        with contextlib.ExitStack() as st:
            A = st.enter_context
            msk4 = A(nc.sbuf_tensor("msk4", [128, 4, 512], F32))
            t4 = p.emit("gpsimd", lambda e: e.memset(msk4[:], 0.0), inc=sGI)
            for k in range(4):
                p.emit("gpsimd", lambda e, k=k: e.affine_select(
                    out=msk4[:, k, :], in_=msk4[:, k, :], compare_op=ALU.is_ge,
                    fill=NEG, base=(k * 128), pattern=[[-1, 512]],
                    channel_multiplier=1), waits=[(sGI, t4)], inc=sGI)
            tok_msk = sGI.n
            Qt_sb = [A(nc.sbuf_tensor(f"Qt{i}", [128, 8, QT], BF16)) for i in range(2)]
            Kt_sb = [A(nc.sbuf_tensor(f"Kt{i}", [128, 8, KT], BF16)) for i in range(3)]
            Vb_sb = [A(nc.sbuf_tensor(f"Vb{i}", [128, 8, D], BF16)) for i in range(3)]
            Oq = A(nc.sbuf_tensor("Oq", [128, 4, D], F32))
            x1t = [A(nc.sbuf_tensor(f"x1t{i}", [128, 4, D], F32)) for i in range(2)]
            p_sb = [A(nc.sbuf_tensor(f"p_sb{i}", [128, KT], BF16)) for i in range(2)]
            pf_sb = A(nc.sbuf_tensor("pf_sb", [128, KT], BF16))
            nmT = A(nc.sbuf_tensor("nmT", [128, 4], F32))
            lT = A(nc.sbuf_tensor("lT", [128, 4], F32))
            alT = A(nc.sbuf_tensor("alT", [128, 4], F32))
            btT = A(nc.sbuf_tensor("btT", [128, 4], F32))
            laccT = A(nc.sbuf_tensor("laccT", [128, 8], F32))
            rmxT = A(nc.sbuf_tensor("rmxT", [128, 2], F32))
            t1T = A(nc.sbuf_tensor("t1T", [128, 1], F32))
            acfl = A(nc.sbuf_tensor("acfl", [128, 1], F32))
            t2T = A(nc.sbuf_tensor("t2T", [128, 1], F32))
            lsT = A(nc.sbuf_tensor("lsT", [128, 1], F32))

            scP = [A(nc.psum_tensor(f"sc{i}", [128, KT], F32)) for i in range(2)]
            pvP = A(nc.psum_tensor("pv", [128, D], F32))
            htB = A(nc.psum_tensor("htB", [128, KT], BF16))

            ktok = [None, None, None]
            kread = [0, 0, 0]
            qtok = [None, None]
            qread = [0, 0]
            x1tok = [None, None]
            x1read = [0, 0]
            x2st = [None, None]

            uses = []
            usebuf = {}
            for _qt in range(nqt):
                _nkt = (_qt * QT + QT + KT - 1) // KT
                for _kt in range(_nkt):
                    usebuf[(_qt, _kt)] = len(uses) % 3
                    uses.append((_qt, _kt))

            def load_kv(n):
                _, kt = uses[n]
                buf = n % 3
                k0 = kt * KT
                tka = p.dma("gpsimd", lambda e, k0=k0, buf=buf: e.dma_start(
                    out=Kt_sb[buf][:],
                    in_=Kf.rearrange("(dc p) k -> p dc k", p=128)[:, :, k0:k0 + KT]),
                    waits=[(sPE, kread[buf])])
                tkb = p.dma("gpsimd", lambda e, k0=k0, buf=buf: e.dma_start(
                    out=Vb_sb[buf][:],
                    in_=Vt_d[k0:k0 + KT, :].rearrange("(kc p) d -> p kc d", p=128)),
                    waits=[(sPE, kread[buf])])
                ktok[buf] = (tka, tkb)

            def load_q(qt):
                par = qt % 2
                q0 = qt * QT
                qtok[par] = p.dma("scalar", lambda e, q0=q0, par=par: e.dma_start(
                    out=Qt_sb[par][:],
                    in_=Qf.rearrange("(dc p) q -> p dc q", p=128)[:, :, q0:q0 + QT]),
                    waits=[(sPE, qread[par])])
                x1tok[par] = p.dma("scalar", lambda e, q0=q0, par=par: e.dma_start(
                    out=x1t[par][:],
                    in_=x1d[q0:q0 + QT, :].rearrange("(qc p) d -> p qc d", p=128)),
                    waits=[(sDV, x1read[par]), x2st[par]])

            load_q(0)
            load_kv(0)
            if len(uses) > 1:
                load_kv(1)
            n_use = 0

            tok_exp = [0, 0]
            tok_rmx = [0, 0]
            tok_ptr = [0, 0]
            tok_pevac = 0
            tok_ostt = 0

            for qt in range(nqt):
                nkt = (qt * QT + QT + KT - 1) // KT
                qpar = qt % 2
                if qt + 1 < nqt:
                    load_q(qt + 1)

                pairs = []
                for kt in range(nkt):
                    for qc in range(4):
                        qpos = qt * QT + qc * 128
                        if kt * KT > qpos + 127:
                            continue
                        pairs.append((kt, qc))

                state = {}

                def emit_lupdate(j):
                    ktj, qcj = pairs[j]
                    khsj = state[(j, "khs")]
                    taj = state[(j, "exp")]
                    firstj = ktj == 0
                    if len(khsj) == 2:
                        p.emit("vector", lambda e, qcj=qcj: e.tensor_tensor(
                            lsT[:], laccT[:, 2 * qcj:2 * qcj + 1],
                            laccT[:, 2 * qcj + 1:2 * qcj + 2], op=ALU.add),
                            waits=[(sAC, taj)])
                    else:
                        kh0 = khsj[0][0]
                        p.emit("vector", lambda e, qcj=qcj, kh0=kh0: e.tensor_copy(
                            lsT[:], laccT[:, 2 * qcj + kh0:2 * qcj + kh0 + 1]),
                            waits=[(sAC, taj)])
                    vpad()
                    if firstj:
                        p.emit("vector", lambda e, qcj=qcj: e.tensor_copy(
                            lT[:, qcj:qcj + 1], lsT[:]))
                    else:
                        p.emit("vector", lambda e, qcj=qcj: e.scalar_tensor_tensor(
                            lT[:, qcj:qcj + 1], lT[:, qcj:qcj + 1],
                            alT[:, qcj:qcj + 1], lsT[:],
                            op0=ALU.mult, op1=ALU.add))

                def pair_front(i):
                    kt, qc = pairs[i]
                    qpos = qt * QT + qc * 128
                    spar = i % 2
                    kpar = usebuf[(qt, kt)]
                    khs = []
                    for kh in range(2):
                        kp = kt * KT + kh * 512
                        if kp > qpos + 127:
                            continue
                        delta = qpos - kp
                        khs.append((kh, None if delta >= 512 else delta // 128))
                    t = 0
                    for kh, dslot in khs:
                        for dc in range(8):
                            t = p.emit(
                                "tensor",
                                lambda e, kh=kh, dc=dc, spar=spar, kpar=kpar,
                                       qpar=qpar, qc=qc: e.matmul(
                                    scP[spar][:, kh * 512:(kh + 1) * 512],
                                    Qt_sb[qpar][:, dc, qc * 128:(qc + 1) * 128],
                                    Kt_sb[kpar][:, dc, kh * 512:(kh + 1) * 512],
                                    start=(dc == 0), stop=(dc == 7)),
                                waits=[qtok[qpar], ktok[kpar][0], ktok[kpar][1],
                                       (sAC, tok_exp[spar]), (sDV, tok_rmx[spar])],
                                inc=sPE if (dc == 7 and kh == khs[-1][0]) else None)
                    state[(i, "sc")] = t
                    kread[kpar] = max(kread[kpar], t)
                    qread[qpar] = max(qread[qpar], t)
                    first = kt == 0
                    for kh, dslot in khs:
                        if dslot is not None:
                            p.emit("vector", lambda e, kh=kh, spar=spar, dslot=dslot:
                                e.tensor_tensor(
                                    scP[spar][:, kh * 512:(kh + 1) * 512],
                                    scP[spar][:, kh * 512:(kh + 1) * 512],
                                    msk4[:, dslot, :], op=ALU.add),
                                waits=[(sPE, t), (sGI, tok_msk)])
                        p.emit("vector", lambda e, kh=kh, spar=spar: e.reduce_max(
                            rmxT[:, kh:kh + 1],
                            scP[spar][:, kh * 512:(kh + 1) * 512],
                            axis=AX.X, negate=True), waits=[(sPE, t)])
                    if i > 0:
                        emit_lupdate(i - 1)
                    else:
                        vpad()
                    if len(khs) == 2:
                        p.emit("vector", lambda e: e.tensor_tensor(
                            t1T[:], rmxT[:, 0:1], rmxT[:, 1:2], op=ALU.min))
                    else:
                        kh0 = khs[0][0]
                        p.emit("vector", lambda e, kh0=kh0: e.tensor_copy(
                            t1T[:], rmxT[:, kh0:kh0 + 1]))
                    vpad()
                    if not first:
                        p.emit("vector", lambda e, qc=qc: e.tensor_tensor(
                            t1T[:], t1T[:], nmT[:, qc:qc + 1], op=ALU.min))
                        vpad()
                        p.emit("vector", lambda e, qc=qc: e.tensor_tensor(
                            t2T[:], t1T[:], nmT[:, qc:qc + 1], op=ALU.subtract))
                    p.emit("vector", lambda e, qc=qc: e.tensor_copy(
                        nmT[:, qc:qc + 1], t1T[:]))
                    tst = p.emit("vector", lambda e, qc=qc: e.tensor_scalar(
                        btT[:, qc:qc + 1], t1T[:], ISQ, None,
                        op0=ALU.mult), inc=sDV)
                    tok_rmx[spar] = tst
                    ta = 0
                    for kh, dslot in khs:
                        ta = p.emit("scalar",
                            lambda e, kh=kh, spar=spar, qc=qc: e.activation(
                                out=p_sb[spar][:, kh * 512:(kh + 1) * 512],
                                in_=scP[spar][:, kh * 512:(kh + 1) * 512],
                                func=AF.Exp, scale=ISQ, bias=btT[:, qc:qc + 1],
                                accum_out=laccT[:, 2 * qc + kh:2 * qc + kh + 1]),
                            waits=[(sDV, tst), (sPE, tok_ptr[spar])], inc=sAC)
                    if not first:
                        p.emit("scalar", lambda e, qc=qc: e.activation(
                            out=alT[:, qc:qc + 1], in_=t2T[:], func=AF.Exp,
                            scale=ISQ), waits=[(sDV, tst)])
                    p.emit("scalar", lambda e, qc=qc: e.activation(
                        out=acfl[:], in_=btT[:, qc:qc + 1], func=AF.Copy))
                    ta = p.emit("scalar", lambda e, qc=qc: e.activation(
                        out=acfl[:], in_=btT[:, qc:qc + 1], func=AF.Copy),
                        inc=sAC)
                    state[(i, "exp")] = ta
                    tok_exp[spar] = ta
                    state[(i, "khs")] = khs

                def pair_back(i):
                    nonlocal tok_pevac, tok_ostt
                    kt, qc = pairs[i]
                    qpos = qt * QT + qc * 128
                    spar = i % 2
                    kpar = usebuf[(qt, kt)]
                    first = kt == 0
                    vis = min(KT, qpos + 128 - kt * KT)
                    kb_max = (vis + 127) // 128
                    texp = state[(i, "exp")]
                    t = 0
                    for kb in range(kb_max):
                        t = p.emit("tensor", lambda e, kb=kb, spar=spar: e.transpose(
                            htB[:, kb * 128:(kb + 1) * 128],
                            p_sb[spar][:, kb * 128:(kb + 1) * 128], ident[:]),
                            waits=[(sAC, texp), (sDV, tok_pevac)],
                            inc=sPE if kb == kb_max - 1 else None)
                    tok_ptr[spar] = t
                    tok_pevac = p.emit("vector", lambda e, kb_max=kb_max: e.tensor_copy(
                        pf_sb[:, 0:kb_max * 128], htB[:, 0:kb_max * 128]),
                        waits=[(sPE, t)], inc=sDV)
                    tpv = 0
                    for dh in range(2):
                        for kb in range(kb_max):
                            tpv = p.emit(
                                "tensor",
                                lambda e, kb=kb, dh=dh, kpar=kpar: e.matmul(
                                    pvP[:, dh * 512:(dh + 1) * 512],
                                    pf_sb[:, kb * 128:(kb + 1) * 128],
                                    Vb_sb[kpar][:, kb, dh * 512:(dh + 1) * 512],
                                    start=(kb == 0), stop=(kb == kb_max - 1)),
                                waits=[(sDV, tok_pevac), (sDV, tok_ostt),
                                       ktok[kpar][1]],
                                inc=sPE if (dh == 1 and kb == kb_max - 1) else None)
                    kread[kpar] = max(kread[kpar], tpv)
                    if first:
                        tok_ostt = p.emit("vector", lambda e, qc=qc: e.tensor_copy(
                            Oq[:, qc, :], pvP[:]), waits=[(sPE, tpv)], inc=sDV)
                    else:
                        tok_ostt = p.emit("vector", lambda e, qc=qc:
                            e.scalar_tensor_tensor(
                                Oq[:, qc, :], Oq[:, qc, :], alT[:, qc:qc + 1],
                                pvP[:], op0=ALU.mult, op1=ALU.add),
                            waits=[(sPE, tpv)], inc=sDV)

                prev_kt = None
                for i in range(len(pairs)):
                    kt, qc = pairs[i]
                    new_use = kt != prev_kt
                    if new_use:
                        if (qt, kt) != uses[n_use]:
                            n_use += 1
                        assert uses[n_use] == (qt, kt)
                        prev_kt = kt
                    pair_front(i)
                    if i > 0:
                        pair_back(i - 1)
                    if new_use and n_use + 2 < len(uses):
                        load_kv(n_use + 2)
                pair_back(len(pairs) - 1)
                emit_lupdate(len(pairs) - 1)

                for qc in range(4):
                    q0 = qt * QT
                    p.emit("vector", lambda e, qc=qc: e.reciprocal(
                        t1T[:], lT[:, qc:qc + 1]))
                    vpad()
                    te = p.emit("vector", lambda e, qc=qc, qpar=qpar:
                        e.scalar_tensor_tensor(
                            x1t[qpar][:, qc, :], Oq[:, qc, :], t1T[:],
                            x1t[qpar][:, qc, :], op0=ALU.mult, op1=ALU.add),
                        waits=[x1tok[qpar]], inc=sDV)
                    x1read[qpar] = te
                    x2st[qpar] = p.dma("gpsimd",
                        lambda e, qc=qc, q0=q0, qpar=qpar: e.dma_start(
                            out=x2d[q0 + qc * 128:q0 + (qc + 1) * 128, :],
                            in_=x1t[qpar][:, qc, :]), waits=[(sDV, te)])
            p.barrier(ALL_SEMS)

        emit_ffn_pass(x2d, out_d, with_qkv=False, tag="c_")

        p.emit("gpsimd", lambda e: e.nop(),
               waits=[(s, s.n) for v in dma_sems.values() for s in v])

        block = E(nc.Block())
        p.replay(block)
    return nc



def _host_inputs(x_c, wq, wk, wv, w13, w2, w13s, w2s, gw, gb, S):
    perm = np.concatenate([np.arange(0, D, 2), np.arange(1, D, 2)])
    f32 = np.float32
    i = np.arange(D // 2, dtype=f32)
    theta = np.power(f32(10000.0), (f32(-2.0) * (i - f32(1.0)) / f32(D))).astype(f32)
    pos = np.arange(S, dtype=f32)
    ang = (pos[:, None] * theta[None, :]).astype(f32)
    cs = np.concatenate([np.cos(ang), np.sin(ang)], axis=1).astype(f32)

    d = {
        "wqT": np.ascontiguousarray(wq[perm].T),
        "wkT": np.ascontiguousarray(wk[perm].T),
        "wvT": np.ascontiguousarray(wv.T),
        "w13T": np.ascontiguousarray(np.transpose(w13, (0, 2, 1))),
        "w2T": np.ascontiguousarray(np.transpose(w2, (0, 2, 1))),
        "w13sT": np.ascontiguousarray(w13s.T),
        "w2sT": np.ascontiguousarray(w2s.T),
        "gwT": np.ascontiguousarray(gw.T),
        "gbb": np.ascontiguousarray(np.broadcast_to(gb, (128, NE))),
        "cs": cs,
    }
    d = {k: v.astype(np.float32) for k, v in d.items()}
    d["x"] = np.ascontiguousarray(x_c, dtype=np.float32)
    return d


def kernel(L0_attn_x, L0_attn_w_q, L0_attn_w_k, L0_attn_w_v,
           L0_ffn_input_ids, L0_ffn_gate_w, L0_ffn_gate_b,
           L0_ffn_w13, L0_ffn_w2, L0_ffn_w13_s, L0_ffn_w2_s):
    x = np.asarray(L0_attn_x, dtype=np.float32)
    B, S, _ = x.shape
    wq = np.asarray(L0_attn_w_q, dtype=np.float32)
    wk = np.asarray(L0_attn_w_k, dtype=np.float32)
    wv = np.asarray(L0_attn_w_v, dtype=np.float32)
    w13 = np.asarray(L0_ffn_w13, dtype=np.float32)
    w2 = np.asarray(L0_ffn_w2, dtype=np.float32)
    w13s = np.asarray(L0_ffn_w13_s, dtype=np.float32)
    w2s = np.asarray(L0_ffn_w2_s, dtype=np.float32)
    gw = np.asarray(L0_ffn_gate_w, dtype=np.float32)
    gb = np.asarray(L0_ffn_gate_b, dtype=np.float32)

    nc = build(S)
    shared = None
    in_maps = []
    for c in range(B):
        d = _host_inputs(x[c], wq, wk, wv, w13, w2, w13s, w2s, gw, gb, S)
        if shared is None:
            shared = {k: v for k, v in d.items() if k != "x"}
        else:
            for k in shared:
                d[k] = shared[k]
        in_maps.append(d)
    res = run_bass_kernel_spmd(nc, in_maps, core_ids=list(range(B)))
    out = np.stack([res.results[c]["out"] for c in range(B)], axis=0)
    return out


# version 07 (final)
# speedup vs baseline: 1.0680x; 1.0680x over previous
import contextlib
import numpy as np
import concourse.bass as bass
import concourse.mybir as mybir
from concourse.bass_utils import run_bass_kernel_spmd

F32 = mybir.dt.float32
BF16 = mybir.dt.bfloat16
U32 = mybir.dt.uint32
AF = mybir.ActivationFunctionType
ALU = mybir.AluOpType
AX = mybir.AxisListType

D = 1024
NE = 16
NXP = 17
INTER = 512
CH = 128
GRP = 1024
QT = 512
KT = 1024
EPS = 1e-5
NEG = -1e9
RSCALE = 2.5
ISQ = 1.0 / 32.0

ENGS = ("sync", "gpsimd", "vector", "scalar", "tensor")


class Sem:
    __slots__ = ("h", "n")

    def __init__(self, h):
        self.h = h
        self.n = 0


class Prog:

    def __init__(self):
        self.items = []
        self.pending = {}
        self.dmaq = {}

    def emit(self, eng, fn, waits=(), inc=None, inc_n=1):
        w = list(self.pending.pop(eng, ()))
        for t in waits:
            if t is None:
                continue
            s, v = t
            if v > 0:
                assert v <= s.n, "wait on future token"
                w.append((s, v))
        self.items.append((eng, fn, tuple(w), (inc, inc_n) if inc is not None else None))
        if inc is not None:
            inc.n += inc_n
            return inc.n
        return None

    def set_dma_sems(self, eng, sems):
        self.dmaq[eng] = [sems, 0]

    def dma(self, eng, fn, waits=()):
        sems, cnt = self.dmaq[eng]
        s = sems[cnt % len(sems)]
        w = list(waits)
        if s.n:
            w.append((s, s.n))
        v = self.emit(eng, fn, w, inc=s, inc_n=16)
        self.dmaq[eng][1] = cnt + 1
        return (s, v)

    def barrier(self, sems):
        vals = [(s, s.n) for s in sems if s.n > 0]
        for eng in ENGS:
            self.pending[eng] = list(self.pending.get(eng, ())) + vals

    def replay(self, block):
        for name in ENGS:
            items = [(fn, w, inc) for (e, fn, w, inc) in self.items if e == name]

            def body(eng, items=items):
                last = {}
                for fn, waits, inc in items:
                    for s, v in waits:
                        if last.get(id(s), 0) < v:
                            eng.wait_ge(s.h, v)
                            last[id(s)] = v
                    ins = fn(eng)
                    if inc is not None:
                        ins.then_inc(inc[0].h, inc[1])

            getattr(block, name)(body)


def build(S, debug=False):
    grp = min(GRP, S)
    assert S % grp == 0
    gch = grp // CH
    ngrp = S // grp
    nqt = S // QT

    nc = bass.Bass("TRN2", target_bir_lowering=False)

    dt = nc.dram_tensor
    x_in = dt("x", [S, D], F32, kind="ExternalInput").ap()
    wqT_d = dt("wqT", [D, D], F32, kind="ExternalInput").ap()
    wkT_d = dt("wkT", [D, D], F32, kind="ExternalInput").ap()
    wvT_d = dt("wvT", [D, D], F32, kind="ExternalInput").ap()
    w13T = dt("w13T", [NE, D, 2 * INTER], F32, kind="ExternalInput").ap()
    w2T = dt("w2T", [NE, INTER, D], F32, kind="ExternalInput").ap()
    w13sT = dt("w13sT", [D, 2 * INTER], F32, kind="ExternalInput").ap()
    w2sT = dt("w2sT", [INTER, D], F32, kind="ExternalInput").ap()
    gwT = dt("gwT", [D, NE], F32, kind="ExternalInput").ap()
    gbb = dt("gbb", [128, NE], F32, kind="ExternalInput").ap()
    cs_in = dt("cs", [S, D], F32, kind="ExternalInput").ap()
    out_d = dt("out", [S, D], F32, kind="ExternalOutput").ap()

    ik = "ExternalOutput" if debug else "Internal"
    dbg = None
    if debug:
        dbg = {
            "u1f": dt("dbg_u1f", [128, 8, min(GRP, S)], BF16, kind="ExternalOutput").ap(),
            "scg": dt("dbg_scg", [128, min(GRP, S) // CH, NE], F32, kind="ExternalOutput").ap(),
            "comb": dt("dbg_comb", [128, min(GRP, S) // CH, NE], F32, kind="ExternalOutput").ap(),
            "yg": dt("dbg_yg", [128, min(GRP, S) // CH, D], F32, kind="ExternalOutput").ap(),
            "mscg": dt("dbg_mscg", [128, min(GRP, S) // CH, NE], F32, kind="ExternalOutput").ap(),
            "deng": dt("dbg_deng", [128, min(GRP, S) // CH], F32, kind="ExternalOutput").ap(),
            "idxg": dt("dbg_idxg", [128, min(GRP, S) // CH, 8], U32, kind="ExternalOutput").ap(),
            "s0": dt("dbg_s0", [128, INTER], BF16, kind="ExternalOutput").ap(),
            "h0": dt("dbg_h0", [128, INTER], BF16, kind="ExternalOutput").ap(),
            "hf0": dt("dbg_hf0", [128, INTER], BF16, kind="ExternalOutput").ap(),
            "y0": dt("dbg_y0", [128, D], F32, kind="ExternalOutput").ap(),
            "w13b": dt("dbg_w13b", [128, 8, D], BF16, kind="ExternalOutput").ap(),
            "gu0": dt("dbg_gu0", [128, D], F32, kind="ExternalOutput").ap(),
            "ye": dt("dbg_ye", [NXP, 128, D], F32, kind="ExternalOutput").ap(),
        }
    wb13 = dt("wb13", [NXP, D, 2 * INTER], BF16, kind="Internal").ap()
    wb2 = dt("wb2", [NXP, INTER, D], BF16, kind="Internal").ap()
    wbqkv = dt("wbqkv", [3, D, D], BF16, kind="Internal").ap()
    x1d = dt("x1d", [S, D], F32, kind=ik).ap()
    x2d = dt("x2d", [S, D], F32, kind=ik).ap()
    Qf = dt("Qf", [D, S], BF16, kind=ik).ap()
    Kf = dt("Kf", [D, S], BF16, kind=ik).ap()
    Vt_d = dt("Vt", [S, D], BF16, kind=ik).ap()

    p = Prog()
    outer = contextlib.ExitStack()
    with outer:
        E = outer.enter_context
        sPE = Sem(E(nc.semaphore("sPE")))
        sDV = Sem(E(nc.semaphore("sDV")))
        sAC = Sem(E(nc.semaphore("sAC")))
        sGI = Sem(E(nc.semaphore("sGI")))
        dma_sems = {}
        for qname, nq in (("sync", 3), ("gpsimd", 4), ("scalar", 3)):
            dma_sems[qname] = [Sem(E(nc.semaphore(f"sQ_{qname}{i}")))
                               for i in range(nq)]
            p.set_dma_sems(qname, dma_sems[qname])
        ALL_SEMS = [sPE, sDV, sAC, sGI] + [s for v in dma_sems.values()
                                           for s in v]

        ident = E(nc.sbuf_tensor("ident", [128, 128], BF16))
        identF = E(nc.sbuf_tensor("identF", [128, 128], F32))
        iota16 = E(nc.sbuf_tensor("iota16", [128, NE], F32))
        gw_sb = E(nc.sbuf_tensor("gw_sb", [128, 8, NE], F32))
        gbb_sb = E(nc.sbuf_tensor("gbb_sb", [128, NE], F32))
        epsT = E(nc.sbuf_tensor("epsT", [128, 1], F32))
        padT = E(nc.sbuf_tensor("padT", [128, 2], F32))

        def vpad():
            for _ in range(4):
                p.emit("vector", lambda e: e.memset(padT[:, 0:1], 0.0))

        t1 = p.emit("gpsimd", lambda e: e.memset(ident[:], 0.0), inc=sGI)
        t2 = p.emit("gpsimd", lambda e: e.memset(identF[:], 0.0), inc=sGI)
        p.emit("gpsimd", lambda e: e.memset(epsT[:], EPS), inc=sGI)

        p.emit("gpsimd", lambda e: e.affine_select(
            out=ident[:], in_=ident[:], compare_op=ALU.not_equal, fill=1.0,
            base=0, pattern=[[-1, 128]], channel_multiplier=1),
            waits=[(sGI, t1)], inc=sGI)
        p.emit("gpsimd", lambda e: e.affine_select(
            out=identF[:], in_=identF[:], compare_op=ALU.not_equal, fill=1.0,
            base=0, pattern=[[-1, 128]], channel_multiplier=1),
            waits=[(sGI, t2)], inc=sGI)
        p.emit("gpsimd", lambda e: e.iota(
            iota16[:], [[1, NE]], base=0, channel_multiplier=0,
            allow_small_or_imprecise_dtypes=True), inc=sGI)
        tok_const = sGI.n

        tok_gbb = p.dma("gpsimd", lambda e: e.dma_start(out=gbb_sb[:], in_=gbb))

        cast_done = []
        with contextlib.ExitStack() as st:
            stF = [st.enter_context(nc.sbuf_tensor(f"stF{i}", [128, 8, D], F32))
                   for i in range(2)]
            stB = [st.enter_context(nc.sbuf_tensor(f"stB{i}", [128, 8, D], BF16))
                   for i in range(2)]

            tok_gw = p.dma("sync", lambda e: e.dma_start(
                out=gw_sb[:],
                in_=gwT.rearrange("(dc p) e -> p dc e", p=128)))

            jobs = []
            for ee in range(NXP):
                src = (w13T[ee] if ee < NE else w13sT).rearrange(
                    "(dc p) i -> p dc i", p=128)
                jobs.append((src, wb13[ee].rearrange("(dc p) i -> p dc i", p=128), 8))
            for ee in range(NXP):
                src = (w2T[ee] if ee < NE else w2sT).rearrange(
                    "(ic p) d -> p ic d", p=128)
                jobs.append((src, wb2[ee].rearrange("(ic p) d -> p ic d", p=128), 4))
            for j, w in enumerate([wqT_d, wkT_d, wvT_d]):
                jobs.append((w.rearrange("(dc p) e -> p dc e", p=128),
                             wbqkv[j].rearrange("(dc p) e -> p dc e", p=128), 8))

            rd_tok = [0, 0]
            st_tok = [None, None]
            for ji, (src, dst, nf) in enumerate(jobs):
                b = ji % 2
                ld = p.dma("sync", lambda e, src=src, nf=nf, b=b: e.dma_start(
                    out=stF[b][:, 0:nf, :], in_=src),
                    waits=[(sDV, rd_tok[b])])
                rd_tok[b] = p.emit("vector", lambda e, nf=nf, b=b: e.tensor_copy(
                    stB[b][:, 0:nf, :], stF[b][:, 0:nf, :]),
                    waits=[ld, st_tok[b]], inc=sDV)
                st_tok[b] = p.dma("gpsimd", lambda e, dst=dst, nf=nf, b=b: e.dma_start(
                    out=dst, in_=stB[b][:, 0:nf, :]),
                    waits=[(sDV, rd_tok[b])])
                cast_done.append(st_tok[b])
        p.barrier(ALL_SEMS)

        def emit_ffn_pass(src_dram, dst_dram, with_qkv, tag, dbg=None):
            with contextlib.ExitStack() as st:
                A = st.enter_context
                xg = A(nc.sbuf_tensor(tag + "xg", [128, 2 * gch, D], F32))
                u1f = A(nc.sbuf_tensor(tag + "u1f", [128, 16, grp], BF16))
                yg = A(nc.sbuf_tensor(tag + "yg", [128, gch, D], F32))
                wbuf = A(nc.sbuf_tensor(tag + "wbuf", [128, 24576], BF16))
                s_bf = A(nc.sbuf_tensor(tag + "s_bf", [128, 2, INTER], BF16))
                h_bf = A(nc.sbuf_tensor(tag + "h_bf", [128, INTER], BF16))
                hf_bf = A(nc.sbuf_tensor(tag + "hf_bf", [128, INTER], BF16))
                comb = A(nc.sbuf_tensor(tag + "comb", [128, 2 * gch, NE], F32))
                rsg = A(nc.sbuf_tensor(tag + "rsg", [128, gch], F32))
                rqg = A(nc.sbuf_tensor(tag + "rqg", [128, gch], F32))
                mvm = A(nc.sbuf_tensor(tag + "mvm", [128, gch], F32))
                mvq = A(nc.sbuf_tensor(tag + "mvq", [128, gch], F32))
                veg = A(nc.sbuf_tensor(tag + "veg", [128, gch], F32))
                jnk = A(nc.sbuf_tensor(tag + "jnk", [128, D], F32))
                u1g32 = A(nc.sbuf_tensor(tag + "u1g32", [128, D], F32))
                lgg = A(nc.sbuf_tensor(tag + "lgg", [128, gch, NE], F32))
                scg = A(nc.sbuf_tensor(tag + "scg", [128, gch, NE], F32))
                big = A(nc.sbuf_tensor(tag + "big", [128, NE], F32))
                m8t = A(nc.sbuf_tensor(tag + "m8t", [128, 8], F32))
                idx8g = A(nc.sbuf_tensor(tag + "idx8g", [128, gch, 8], U32))
                idx8f = A(nc.sbuf_tensor(tag + "idx8f", [128, 8], F32))
                mscg = A(nc.sbuf_tensor(tag + "mscg", [128, gch, NE], F32))
                mskt = A(nc.sbuf_tensor(tag + "mskt", [128, NE], F32))
                tmp16 = A(nc.sbuf_tensor(tag + "tmp16", [128, NE], F32))
                msk2 = A(nc.sbuf_tensor(tag + "msk2", [128, NE], F32))
                msk3 = A(nc.sbuf_tensor(tag + "msk3", [128, NE], F32))
                deng = A(nc.sbuf_tensor(tag + "deng", [128, gch], F32))
                if with_qkv:
                    cs_sb = [A(nc.sbuf_tensor(tag + "cs_sb0", [128, D], F32))]
                    cs_sb = cs_sb + cs_sb
                    rt1 = jnk[:, 0:INTER]
                    rt2 = jnk[:, INTER:2 * INTER]
                    rot = A(nc.sbuf_tensor(tag + "rot", [128, D], BF16))
                    fstage = A(nc.sbuf_tensor(tag + "fstage", [128, D], BF16))
                    vstage = A(nc.sbuf_tensor(tag + "vstage", [128, D], BF16))

                guP = [A(nc.psum_tensor(tag + "guA", [128, D], F32)),
                       A(nc.psum_tensor(tag + "guB", [128, D], F32))]
                zzP = A(nc.psum_tensor(tag + "zz", [128, D], F32))
                htP = A(nc.psum_tensor(tag + "ht", [128, D], BF16))
                spP = A(nc.psum_tensor(tag + "sp", [128, 512], F32))

                def w13buf(par):
                    return wbuf[:, par * 8192:(par + 1) * 8192].rearrange(
                        "p (dc i) -> p dc i", i=D)

                def w2buf(par):
                    return wbuf[:, 16384 + par * 4096:16384 + (par + 1) * 4096] \
                        .rearrange("p (ic d) -> p ic d", d=D)

                wq_b = wbuf[:, 0:8192].rearrange("p (dc e) -> p dc e", e=D)
                wk_b = wbuf[:, 8192:16384].rearrange("p (dc e) -> p dc e", e=D)
                wv_b = wbuf[:, 16384:24576].rearrange("p (dc e) -> p dc e", e=D)

                last_read = {"w13_0": 0, "w13_1": 0, "w2_0": 0, "w2_1": 0}
                wtok = {}
                xg_freed = [None, None]
                u1f_freed = [0, 0]
                sp_free = [0]
                ht_free = [0]
                gu_guard = [0, 0]
                sil_guard = [0, 0]
                zz_guard = [0]
                pre_tok = {}

                def load_weights(g, e):
                    par = e % 2
                    t1 = p.dma("sync", lambda eng, e=e, par=par: eng.dma_start(
                        out=w13buf(par),
                        in_=wb13[e].rearrange("(dc p) i -> p dc i", p=128)),
                        waits=[(sPE, last_read[f"w13_{par}"]),
                               cast_done[e]])
                    t2 = p.dma("sync", lambda eng, e=e, par=par: eng.dma_start(
                        out=w2buf(par),
                        in_=wb2[e].rearrange("(ic p) d -> p ic d", p=128)),
                        waits=[(sPE, last_read[f"w2_{par}"]),
                               cast_done[NXP + e]])
                    wtok[(g, e)] = (t1, t2)

                def load_qkv(g):
                    toks = []
                    for j in range(3):
                        dst = [wq_b, wk_b, wv_b][j]
                        regions = [["w13_0"], ["w13_1"], ["w2_0", "w2_1"]][j]
                        t = p.dma("sync", lambda eng, j=j, dst=dst: eng.dma_start(
                            out=dst,
                            in_=wbqkv[j].rearrange("(dc p) e -> p dc e", p=128)),
                            waits=[(sPE, max(last_read[r] for r in regions)),
                                   cast_done[2 * NXP + j]])
                        toks.append(t)
                    wtok[("qkv", g)] = toks

                def emit_stats(xoff, xg_ld):
                    for c in range(gch):
                        p.emit("vector", lambda e, c=c: e.reduce_sum(
                            rsg[:, c:c + 1], xg[:, xoff + c, :], axis=AX.X),
                            waits=[xg_ld])
                    for c in range(gch):
                        p.emit("vector", lambda e, c=c: e.scalar_tensor_tensor(
                            jnk[:], xg[:, xoff + c, :], 1.0, xg[:, xoff + c, :],
                            op0=ALU.mult, op1=ALU.mult,
                            accum_out=rqg[:, c:c + 1]))
                    for _ in range(3):
                        p.emit("vector", lambda e: e.memset(padT[:, 0:1], 0.0))
                    tdv = p.emit("vector", lambda e: e.memset(padT[:, 1:2], 0.0),
                                 inc=sDV)
                    p.emit("scalar", lambda e: e.activation(
                        out=mvm[:], in_=rsg[:], func=AF.Copy, scale=1.0 / D),
                        waits=[(sDV, tdv)])
                    tac = p.emit("scalar", lambda e: e.activation(
                        out=mvq[:], in_=rqg[:], func=AF.Copy, scale=1.0 / D),
                        inc=sAC)
                    p.emit("vector", lambda e: e.tensor_tensor(
                        rsg[:], mvm[:], mvm[:], op=ALU.mult),
                        waits=[(sAC, tac)])
                    vpad()
                    tdv2 = p.emit("vector", lambda e: e.tensor_tensor(
                        veg[:], mvq[:], rsg[:], op=ALU.subtract), inc=sDV)
                    tac2 = p.emit("scalar", lambda e: e.activation(
                        out=veg[:], in_=veg[:], func=AF.Sqrt, bias=epsT[:]),
                        waits=[(sDV, tdv2)], inc=sAC)
                    p.emit("vector", lambda e: e.reciprocal(veg[:], veg[:]),
                           waits=[(sAC, tac2)])
                    vpad()

                def emit_ln_chunk(xoff, uoff, c, gate, piecewise=False):
                    tn = p.emit("vector", lambda e, c=c: e.tensor_scalar(
                        jnk[:], xg[:, xoff + c, :], mvm[:, c:c + 1],
                        veg[:, c:c + 1], op0=ALU.subtract, op1=ALU.mult),
                        inc=sDV)
                    if piecewise:
                        yield
                    ev = 0
                    for rr in range(2):
                        tt = 0
                        for b in range(4):
                            dc = rr * 4 + b
                            tt = p.emit("tensor", lambda e, dc=dc, b=b:
                                e.transpose(
                                    spP[:, b * 128:(b + 1) * 128],
                                    jnk[:, dc * 128:(dc + 1) * 128],
                                    identF[:]),
                                waits=[(sDV, tn), (sDV, sp_free[0]),
                                       (sGI, tok_const)],
                                inc=sPE if b == 3 else None)
                        ev = p.emit("vector", lambda e, c=c, rr=rr:
                            e.tensor_copy(
                                u1f[:, uoff + rr * 4:uoff + (rr + 1) * 4,
                                    c * 128:(c + 1) * 128],
                                spP[:].rearrange("p (b t) -> p b t", t=128)),
                            waits=[(sPE, tt)], inc=sDV)
                        sp_free[0] = ev
                        if gate:
                            ev = p.emit("vector", lambda e, rr=rr:
                                e.tensor_copy(
                                    u1g32[:, rr * 512:(rr + 1) * 512],
                                    spP[:]),
                                inc=sDV)
                            sp_free[0] = ev
                    if piecewise:
                        yield
                    if gate:
                        tg = 0
                        for dc in range(8):
                            tg = p.emit("tensor", lambda e, dc=dc: e.matmul(
                                spP[:, 0:NE],
                                u1g32[:, dc * 128:(dc + 1) * 128],
                                gw_sb[:, dc, :],
                                start=(dc == 0), stop=(dc == 7)),
                                waits=[(sDV, ev), tok_gw],
                                inc=sPE if dc == 7 else None)
                        ev = p.emit("vector", lambda e, c=c: e.tensor_copy(
                            lgg[:, c, :], spP[:, 0:NE]),
                            waits=[(sPE, tg)], inc=sDV)
                        sp_free[0] = ev
                    yield ev

                def emit_routing(coff):
                    tl = sDV.n
                    tsg = p.emit("scalar", lambda e: e.activation(
                        out=scg[:].rearrange("p c e -> p (c e)"),
                        in_=lgg[:].rearrange("p c e -> p (c e)"),
                        func=AF.Sigmoid), waits=[(sDV, tl)], inc=sAC)
                    for c in range(gch):
                        p.emit("vector", lambda e, c=c: e.tensor_tensor(
                            big[:], scg[:, c, :], gbb_sb[:], op=ALU.add),
                            waits=[(sAC, tsg), tok_gbb])
                        vpad()
                        p.emit("vector", lambda e: e.max(m8t[:], big[:]))
                        vpad()
                        p.emit("vector", lambda e, c=c: e.max_index(
                            idx8g[:, c, :], m8t[:], big[:]))
                    for c in range(gch):
                        p.emit("vector", lambda e, c=c: e.tensor_copy(
                            idx8f[:], idx8g[:, c, :]))
                        vpad()
                        for j in range(4):
                            dst = [mskt, tmp16, msk2, msk3][j]
                            p.emit("vector", lambda e, j=j, dst=dst: e.tensor_scalar(
                                dst[:], iota16[:], idx8f[:, j:j + 1], None,
                                op0=ALU.is_equal), waits=[(sGI, tok_const)])
                        vpad()
                        p.emit("vector", lambda e: e.tensor_tensor(
                            mskt[:], mskt[:], tmp16[:], op=ALU.add))
                        p.emit("vector", lambda e: e.tensor_tensor(
                            msk2[:], msk2[:], msk3[:], op=ALU.add))
                        vpad()
                        p.emit("vector", lambda e: e.tensor_tensor(
                            mskt[:], mskt[:], msk2[:], op=ALU.add))
                        vpad()
                        p.emit("vector", lambda e, c=c: e.tensor_tensor(
                            mscg[:, c, :], mskt[:], scg[:, c, :], op=ALU.mult))
                        vpad()
                        p.emit("vector", lambda e, c=c: e.reduce_sum(
                            deng[:, c:c + 1], mscg[:, c, :], axis=AX.X))
                    vpad()
                    vpad()
                    p.emit("vector", lambda e: e.reciprocal(deng[:], deng[:]))
                    vpad()
                    t = 0
                    for c in range(gch):
                        t = p.emit("vector", lambda e, c=c, coff=coff: e.tensor_scalar(
                            comb[:, coff + c, :], mscg[:, c, :],
                            deng[:, c:c + 1], RSCALE,
                            op0=ALU.mult, op1=ALU.mult), inc=sDV)
                    return t

                def prelude(g):
                    par = g % 2
                    xoff = par * gch
                    xg_ld = p.dma("gpsimd", lambda e, g=g, xoff=xoff: e.dma_start(
                        out=xg[:, xoff:xoff + gch, :],
                        in_=src_dram[g * grp:(g + 1) * grp, :].rearrange(
                            "(c p) d -> p c d", p=128)),
                        waits=[xg_freed[par]])
                    yield
                    emit_stats(xoff, xg_ld)
                    yield
                    ev = 0
                    for c in range(gch):
                        for r in emit_ln_chunk(xoff, par * 8, c, gate=True,
                                               piecewise=True):
                            if r is not None:
                                ev = r
                            yield
                    ctok = emit_routing(par * gch)
                    pre_tok[g] = {"ev": ev, "comb": ctok, "xg_ld": xg_ld}
                    yield

                def run_prelude_to_end(gen):
                    for _ in gen:
                        pass

                load_weights(0, 0)
                load_weights(0, 1)
                run_prelude_to_end(prelude(0))

                for g in range(ngrp):
                    g0 = g * grp
                    par = g % 2
                    xoff = par * gch
                    ev = pre_tok[g]["ev"]
                    xg_ld = pre_tok[g]["xg_ld"]


                    units = [(e, c) for e in range(NXP) for c in range(gch)]
                    tok_gu = {}
                    tok_tr = {}
                    tok_z = {}
                    tok_h = {-2: gu_guard[0], -1: gu_guard[1]}
                    tok_ev = {-1: ht_free[0]}
                    tok_sil = {-2: sil_guard[0], -1: sil_guard[1]}
                    tok_y = {-1: zz_guard[0]}

                    def pe_trans(u):
                        t = 0
                        for ib in range(4):
                            t = p.emit("tensor", lambda e, ib=ib: e.transpose(
                                htP[:, ib * 128:(ib + 1) * 128],
                                h_bf[:, ib * 128:(ib + 1) * 128], ident[:]),
                                waits=[(sDV, tok_h[u]), (sDV, tok_ev[u - 1])],
                                inc=sPE if ib == 3 else None)
                        tok_tr[u] = t

                    def dv_evac(u):
                        tok_ev[u] = p.emit("vector", lambda e: e.tensor_copy(
                            hf_bf[:], htP[:, 0:512]),
                            waits=[(sPE, tok_tr[u])], inc=sDV)

                    def pe_gu(u):
                        eu, cu = units[u]
                        bpar = u % 2
                        w13t = wtok[(g, eu)][0]
                        t = 0
                        for ih in range(2):
                            for dc in range(8):
                                t = p.emit(
                                    "tensor",
                                    lambda e, cu=cu, dc=dc, ih=ih, bpar=bpar,
                                           eu=eu, xoff=xoff, par=par:
                                    e.matmul(
                                        guP[bpar][:, ih * 512:(ih + 1) * 512],
                                        u1f[:, par * 8 + dc,
                                            cu * 128:(cu + 1) * 128],
                                        w13buf(eu % 2)[:, dc,
                                                       ih * 512:(ih + 1) * 512],
                                        start=(dc == 0), stop=(dc == 7)),
                                    waits=[w13t,
                                           (sDV, tok_h[u - 2]),
                                           (sAC, tok_sil[u - 2]),
                                           (sDV, ev)],
                                    inc=sPE if (ih == 1 and dc == 7) else None)
                        tok_gu[u] = t
                        last_read[f"w13_{eu % 2}"] = t
                        u1f_freed[par] = t

                    def pe_z(u):
                        eu, cu = units[u]
                        w2t = wtok[(g, eu)][1]
                        t = 0
                        for dh in range(2):
                            for ic in range(4):
                                t = p.emit(
                                    "tensor",
                                    lambda e, ic=ic, dh=dh, eu=eu: e.matmul(
                                        zzP[:, dh * 512:(dh + 1) * 512],
                                        hf_bf[:, ic * 128:(ic + 1) * 128],
                                        w2buf(eu % 2)[:, ic,
                                                      dh * 512:(dh + 1) * 512],
                                        start=(ic == 0), stop=(ic == 3)),
                                    waits=[w2t, (sDV, tok_ev[u]),
                                           (sDV, tok_y[u - 1])],
                                    inc=sPE if (dh == 1 and ic == 3) else None)
                        tok_z[u] = t
                        last_read[f"w2_{eu % 2}"] = t

                    def ac_silu(u):
                        bpar = u % 2
                        tok_sil[u] = p.emit("scalar", lambda e, bpar=bpar:
                            e.activation(
                                out=s_bf[:, bpar, :], in_=guP[bpar][:, 0:512],
                                func=AF.Silu),
                            waits=[(sPE, tok_gu[u]), (sDV, tok_h[u - 2])],
                            inc=sAC)

                    def dv_h(u):
                        bpar = u % 2
                        tok_h[u] = p.emit("vector", lambda e, bpar=bpar:
                            e.tensor_tensor(
                                h_bf[:], guP[bpar][:, 512:1024],
                                s_bf[:, bpar, :], op=ALU.mult),
                            waits=[(sAC, tok_sil[u]), (sPE, tok_gu[u])], inc=sDV)

                    def dv_y(u):
                        eu, cu = units[u]
                        coff = par * gch
                        if eu == 0:
                            tok_y[u] = p.emit("vector", lambda e, cu=cu, eu=eu,
                                coff=coff: e.tensor_scalar(
                                    yg[:, cu, :], zzP[:],
                                    comb[:, coff + cu, eu:eu + 1],
                                    None, op0=ALU.mult),
                                waits=[(sPE, tok_z[u])], inc=sDV)
                        elif eu < NE:
                            tok_y[u] = p.emit("vector", lambda e, cu=cu, eu=eu,
                                coff=coff: e.scalar_tensor_tensor(
                                    yg[:, cu, :], zzP[:],
                                    comb[:, coff + cu, eu:eu + 1],
                                    yg[:, cu, :], op0=ALU.mult, op1=ALU.add),
                                waits=[(sPE, tok_z[u])], inc=sDV)
                        else:
                            tok_y[u] = p.emit("vector", lambda e, cu=cu:
                                e.scalar_tensor_tensor(
                                    yg[:, cu, :], zzP[:], 1.0,
                                    yg[:, cu, :], op0=ALU.mult, op1=ALU.add),
                                waits=[(sPE, tok_z[u])], inc=sDV)

                    ngen = prelude(g + 1) if g + 1 < ngrp else None
                    gen_done = ngen is None
                    step = max(1, len(units) // 30)

                    for u in range(len(units)):
                        eu, cu = units[u]
                        if u > 0:
                            pe_trans(u - 1)
                            dv_evac(u - 1)
                        pe_gu(u)
                        if u > 0:
                            pe_z(u - 1)
                        if cu == 0:
                            if eu + 1 < NXP and (g, eu + 1) not in wtok:
                                load_weights(g, eu + 1)
                            elif (eu == NXP - 1 and not with_qkv
                                  and g + 1 < ngrp):
                                load_weights(g + 1, 1)
                        ac_silu(u)
                        dv_h(u)
                        if u > 0:
                            dv_y(u - 1)
                        if (not gen_done and u >= 8 and (u - 8) % step == 0):
                            try:
                                next(ngen)
                            except StopIteration:
                                gen_done = True
                    u_last = len(units) - 1
                    pe_trans(u_last)
                    dv_evac(u_last)
                    pe_z(u_last)
                    dv_y(u_last)
                    if not gen_done:
                        run_prelude_to_end(ngen)
                    if with_qkv:
                        load_qkv(g)
                    elif g + 1 < ngrp:
                        load_weights(g + 1, 0)

                    ep = 0
                    for c in range(gch):
                        ep = p.emit("vector", lambda e, c=c, xoff=xoff:
                            e.scalar_tensor_tensor(
                                xg[:, xoff + c, :], yg[:, c, :], 0.5,
                                xg[:, xoff + c, :],
                                op0=ALU.mult, op1=ALU.add),
                            waits=[(sDV, tok_y[(NXP - 1) * gch + c])], inc=sDV)

                    if not with_qkv:
                        xg_freed[par] = p.dma("gpsimd",
                            lambda e, g0=g0, xoff=xoff: e.dma_start(
                                out=dst_dram[g0:g0 + grp, :].rearrange(
                                    "(c p) d -> p c d", p=128),
                                in_=xg[:, xoff:xoff + gch, :]),
                            waits=[(sDV, ep)])
                        u1f_freed[par] = max(u1f_freed[par], tok_gu[u_last])
                        gu_guard[0] = tok_h[u_last - 1]
                        gu_guard[1] = tok_h[u_last]
                        sil_guard[0] = tok_sil[u_last - 1]
                        sil_guard[1] = tok_sil[u_last]
                        zz_guard[0] = tok_y[u_last]
                        continue

                    emit_stats(xoff, None)
                    ev2 = 0
                    for c in range(gch):
                        for r in emit_ln_chunk(xoff, par * 8, c, gate=False):
                            if r is not None:
                                ev2 = r
                    qkvt = wtok[("qkv", g)]
                    ac_g = sAC.n
                    rope_done = [0, 0]
                    psum_free = [0, 0, 0]
                    ht_guard = tok_ev[u_last]
                    fst_store = None
                    vst_store = None
                    for c in range(gch):
                        cld = p.dma("scalar", lambda e, g0=g0, c=c:
                            e.dma_start(
                                out=cs_sb[0][:],
                                in_=cs_in[g0 + c * 128:g0 + (c + 1) * 128, :]),
                            waits=[(sDV, max(rope_done))])
                        mt = [0, 0, 0]
                        for j in range(3):
                            psum = [guP[0], guP[1], zzP][j]
                            wb = [wq_b, wk_b, wv_b][j]
                            wt = qkvt[j]
                            t = 0
                            for eh in range(2):
                                for dc in range(8):
                                    t = p.emit(
                                        "tensor",
                                        lambda e, c=c, dc=dc, eh=eh, psum=psum,
                                               wb=wb, par=par: e.matmul(
                                            psum[:, eh * 512:(eh + 1) * 512],
                                            u1f[:, par * 8 + dc,
                                                c * 128:(c + 1) * 128],
                                            wb[:, dc, eh * 512:(eh + 1) * 512],
                                            start=(dc == 0), stop=(dc == 7)),
                                        waits=[wt, (sDV, ev2),
                                               (sDV, psum_free[j]), (sAC, ac_g)],
                                        inc=sPE if (eh == 1 and dc == 7) else None)
                            mt[j] = t
                            u1f_freed[par] = t
                        rtoks = [0, 0]
                        tts = [0, 0]
                        for j in range(2):
                            psum = [guP[0], guP[1]][j]
                            p.emit("vector", lambda e, psum=psum, j=j:
                                e.tensor_tensor(
                                    rt1, psum[:, 0:512],
                                    cs_sb[0][:, 0:512], op=ALU.mult),
                                waits=[(sPE, mt[j]), cld,
                                       (sPE, tts[0] if j == 1 else 0)])
                            p.emit("vector", lambda e, psum=psum:
                                e.tensor_tensor(
                                    rt2, psum[:, 512:1024],
                                    cs_sb[0][:, 512:1024], op=ALU.mult))
                            p.emit("vector", lambda e: e.tensor_tensor(
                                rot[:, 0:512], rt1, rt2, op=ALU.add))
                            p.emit("vector", lambda e, psum=psum:
                                e.tensor_tensor(
                                    rt1, psum[:, 512:1024],
                                    cs_sb[0][:, 0:512], op=ALU.mult))
                            p.emit("vector", lambda e, psum=psum:
                                e.tensor_tensor(
                                    rt2, psum[:, 0:512],
                                    cs_sb[0][:, 512:1024], op=ALU.mult))
                            pq = p.emit("vector", lambda e: e.tensor_tensor(
                                rot[:, 512:1024], rt1, rt2,
                                op=ALU.subtract), inc=sDV)
                            rtoks[j] = pq
                            psum_free[j] = pq
                            tt = 0
                            for dc in range(8):
                                tt = p.emit("tensor", lambda e, dc=dc:
                                    e.transpose(
                                        htP[:, dc * 128:(dc + 1) * 128],
                                        rot[:, dc * 128:(dc + 1) * 128],
                                        ident[:]),
                                    waits=[(sDV, pq), (sDV, ht_guard)],
                                    inc=sPE if dc == 7 else None)
                            tts[j] = tt
                            fe = p.emit("vector", lambda e: e.tensor_copy(
                                fstage[:], htP[:]),
                                waits=[(sPE, tt), fst_store], inc=sDV)
                            ht_guard = fe
                            dstf = Qf if j == 0 else Kf
                            fst_store = p.dma("gpsimd",
                                lambda e, dstf=dstf, g0=g0, c=c: e.dma_start(
                                    out=dstf.rearrange(
                                        "(dc p) q -> p dc q", p=128)[
                                        :, :, g0 + c * 128:g0 + (c + 1) * 128],
                                    in_=fstage[:].rearrange(
                                        "p (dc t) -> p dc t", t=128)),
                                waits=[(sDV, fe)])
                            if j == 0:
                                pv = p.emit("vector", lambda e: e.tensor_copy(
                                    vstage[:], zzP[:]),
                                    waits=[(sPE, mt[2]), vst_store], inc=sDV)
                                psum_free[2] = pv
                                vst_store = p.dma("gpsimd",
                                    lambda e, g0=g0, c=c: e.dma_start(
                                        out=Vt_d[g0 + c * 128:
                                                 g0 + (c + 1) * 128, :],
                                        in_=vstage[:]),
                                    waits=[(sDV, pv)])
                        rope_done = [rtoks[0], rtoks[1]]
                    ht_free[0] = ht_guard
                    gu_guard[0] = psum_free[0]
                    gu_guard[1] = psum_free[1]
                    sil_guard[0] = sil_guard[1] = sAC.n
                    zz_guard[0] = psum_free[2]
                    last_read["w13_0"] = sPE.n
                    last_read["w13_1"] = sPE.n
                    last_read["w2_0"] = sPE.n
                    last_read["w2_1"] = sPE.n
                    if g + 1 < ngrp:
                        load_weights(g + 1, 0)
                        load_weights(g + 1, 1)
                    xg_freed[par] = p.dma("gpsimd",
                        lambda e, g0=g0, xoff=xoff: e.dma_start(
                            out=dst_dram[g0:g0 + grp, :].rearrange(
                                "(c p) d -> p c d", p=128),
                            in_=xg[:, xoff:xoff + gch, :]),
                        waits=[(sDV, ep)])
            p.barrier(ALL_SEMS)

        emit_ffn_pass(x_in, x1d, with_qkv=True, tag="a_", dbg=dbg)

        with contextlib.ExitStack() as st:
            A = st.enter_context
            msk4 = A(nc.sbuf_tensor("msk4", [128, 4, 512], F32))
            t4 = p.emit("gpsimd", lambda e: e.memset(msk4[:], 0.0), inc=sGI)
            for k in range(4):
                p.emit("gpsimd", lambda e, k=k: e.affine_select(
                    out=msk4[:, k, :], in_=msk4[:, k, :], compare_op=ALU.is_ge,
                    fill=NEG, base=(k * 128), pattern=[[-1, 512]],
                    channel_multiplier=1), waits=[(sGI, t4)], inc=sGI)
            tok_msk = sGI.n
            Qt_sb = [A(nc.sbuf_tensor(f"Qt{i}", [128, 8, QT], BF16)) for i in range(2)]
            Kt_sb = [A(nc.sbuf_tensor(f"Kt{i}", [128, 8, KT], BF16)) for i in range(3)]
            Vb_sb = [A(nc.sbuf_tensor(f"Vb{i}", [128, 8, D], BF16)) for i in range(3)]
            Oq = A(nc.sbuf_tensor("Oq", [128, 4, D], F32))
            x1t = [A(nc.sbuf_tensor(f"x1t{i}", [128, 4, D], F32)) for i in range(2)]
            p_sb = [A(nc.sbuf_tensor(f"p_sb{i}", [128, KT], BF16)) for i in range(2)]
            pf_sb = A(nc.sbuf_tensor("pf_sb", [128, KT], BF16))
            nmT = A(nc.sbuf_tensor("nmT", [128, 4], F32))
            lT = A(nc.sbuf_tensor("lT", [128, 4], F32))
            alT = A(nc.sbuf_tensor("alT", [128, 4], F32))
            btT = A(nc.sbuf_tensor("btT", [128, 4], F32))
            laccT = A(nc.sbuf_tensor("laccT", [128, 8], F32))
            rmxT = A(nc.sbuf_tensor("rmxT", [128, 2], F32))
            t1T = A(nc.sbuf_tensor("t1T", [128, 1], F32))
            acfl = A(nc.sbuf_tensor("acfl", [128, 1], F32))
            t2T = A(nc.sbuf_tensor("t2T", [128, 1], F32))
            lsT = A(nc.sbuf_tensor("lsT", [128, 1], F32))

            scP = [A(nc.psum_tensor(f"sc{i}", [128, KT], F32)) for i in range(2)]
            pvP = A(nc.psum_tensor("pv", [128, D], F32))
            htB = A(nc.psum_tensor("htB", [128, KT], BF16))

            ktok = [None, None, None]
            kread = [0, 0, 0]
            qtok = [None, None]
            qread = [0, 0]
            x1tok = [None, None]
            x1read = [0, 0]
            x2st = [None, None]

            uses = []
            usebuf = {}
            for _qt in range(nqt):
                _nkt = (_qt * QT + QT + KT - 1) // KT
                for _kt in range(_nkt):
                    usebuf[(_qt, _kt)] = len(uses) % 3
                    uses.append((_qt, _kt))

            def load_kv(n):
                _, kt = uses[n]
                buf = n % 3
                k0 = kt * KT
                tka = p.dma("gpsimd", lambda e, k0=k0, buf=buf: e.dma_start(
                    out=Kt_sb[buf][:],
                    in_=Kf.rearrange("(dc p) k -> p dc k", p=128)[:, :, k0:k0 + KT]),
                    waits=[(sPE, kread[buf])])
                tkb = p.dma("gpsimd", lambda e, k0=k0, buf=buf: e.dma_start(
                    out=Vb_sb[buf][:],
                    in_=Vt_d[k0:k0 + KT, :].rearrange("(kc p) d -> p kc d", p=128)),
                    waits=[(sPE, kread[buf])])
                ktok[buf] = (tka, tkb)

            def load_q(qt):
                par = qt % 2
                q0 = qt * QT
                qtok[par] = p.dma("scalar", lambda e, q0=q0, par=par: e.dma_start(
                    out=Qt_sb[par][:],
                    in_=Qf.rearrange("(dc p) q -> p dc q", p=128)[:, :, q0:q0 + QT]),
                    waits=[(sPE, qread[par])])
                x1tok[par] = p.dma("scalar", lambda e, q0=q0, par=par: e.dma_start(
                    out=x1t[par][:],
                    in_=x1d[q0:q0 + QT, :].rearrange("(qc p) d -> p qc d", p=128)),
                    waits=[(sDV, x1read[par]), x2st[par]])

            load_q(0)
            load_kv(0)
            if len(uses) > 1:
                load_kv(1)
            n_use = 0

            tok_exp = [0, 0]
            tok_rmx = [0, 0]
            tok_ptr = [0, 0]
            tok_pevac = 0
            tok_ostt = 0

            for qt in range(nqt):
                nkt = (qt * QT + QT + KT - 1) // KT
                qpar = qt % 2
                if qt + 1 < nqt:
                    load_q(qt + 1)

                pairs = []
                for kt in range(nkt):
                    for qc in range(4):
                        qpos = qt * QT + qc * 128
                        if kt * KT > qpos + 127:
                            continue
                        pairs.append((kt, qc))

                state = {}

                def emit_lupdate(j):
                    ktj, qcj = pairs[j]
                    khsj = state[(j, "khs")]
                    taj = state[(j, "exp")]
                    firstj = ktj == 0
                    if len(khsj) == 2:
                        p.emit("vector", lambda e, qcj=qcj: e.tensor_tensor(
                            lsT[:], laccT[:, 2 * qcj:2 * qcj + 1],
                            laccT[:, 2 * qcj + 1:2 * qcj + 2], op=ALU.add),
                            waits=[(sAC, taj)])
                    else:
                        kh0 = khsj[0][0]
                        p.emit("vector", lambda e, qcj=qcj, kh0=kh0: e.tensor_copy(
                            lsT[:], laccT[:, 2 * qcj + kh0:2 * qcj + kh0 + 1]),
                            waits=[(sAC, taj)])
                    vpad()
                    if firstj:
                        p.emit("vector", lambda e, qcj=qcj: e.tensor_copy(
                            lT[:, qcj:qcj + 1], lsT[:]))
                    else:
                        p.emit("vector", lambda e, qcj=qcj: e.scalar_tensor_tensor(
                            lT[:, qcj:qcj + 1], lT[:, qcj:qcj + 1],
                            alT[:, qcj:qcj + 1], lsT[:],
                            op0=ALU.mult, op1=ALU.add))

                def pair_front(i):
                    kt, qc = pairs[i]
                    qpos = qt * QT + qc * 128
                    spar = i % 2
                    kpar = usebuf[(qt, kt)]
                    khs = []
                    for kh in range(2):
                        kp = kt * KT + kh * 512
                        if kp > qpos + 127:
                            continue
                        delta = qpos - kp
                        khs.append((kh, None if delta >= 512 else delta // 128))
                    t = 0
                    for kh, dslot in khs:
                        for dc in range(8):
                            t = p.emit(
                                "tensor",
                                lambda e, kh=kh, dc=dc, spar=spar, kpar=kpar,
                                       qpar=qpar, qc=qc: e.matmul(
                                    scP[spar][:, kh * 512:(kh + 1) * 512],
                                    Qt_sb[qpar][:, dc, qc * 128:(qc + 1) * 128],
                                    Kt_sb[kpar][:, dc, kh * 512:(kh + 1) * 512],
                                    start=(dc == 0), stop=(dc == 7)),
                                waits=[qtok[qpar], ktok[kpar][0], ktok[kpar][1],
                                       (sAC, tok_exp[spar]), (sDV, tok_rmx[spar])],
                                inc=sPE if (dc == 7 and kh == khs[-1][0]) else None)
                    state[(i, "sc")] = t
                    kread[kpar] = max(kread[kpar], t)
                    qread[qpar] = max(qread[qpar], t)
                    first = kt == 0
                    for kh, dslot in khs:
                        if dslot is not None:
                            p.emit("vector", lambda e, kh=kh, spar=spar, dslot=dslot:
                                e.tensor_tensor(
                                    scP[spar][:, kh * 512:(kh + 1) * 512],
                                    scP[spar][:, kh * 512:(kh + 1) * 512],
                                    msk4[:, dslot, :], op=ALU.add),
                                waits=[(sPE, t), (sGI, tok_msk)])
                        p.emit("vector", lambda e, kh=kh, spar=spar: e.reduce_max(
                            rmxT[:, kh:kh + 1],
                            scP[spar][:, kh * 512:(kh + 1) * 512],
                            axis=AX.X, negate=True), waits=[(sPE, t)])
                    if i > 0:
                        emit_lupdate(i - 1)
                    else:
                        vpad()
                    if len(khs) == 2:
                        p.emit("vector", lambda e: e.tensor_tensor(
                            t1T[:], rmxT[:, 0:1], rmxT[:, 1:2], op=ALU.min))
                    else:
                        kh0 = khs[0][0]
                        p.emit("vector", lambda e, kh0=kh0: e.tensor_copy(
                            t1T[:], rmxT[:, kh0:kh0 + 1]))
                    vpad()
                    if not first:
                        p.emit("vector", lambda e, qc=qc: e.tensor_tensor(
                            t1T[:], t1T[:], nmT[:, qc:qc + 1], op=ALU.min))
                        vpad()
                        p.emit("vector", lambda e, qc=qc: e.tensor_tensor(
                            t2T[:], t1T[:], nmT[:, qc:qc + 1], op=ALU.subtract))
                    p.emit("vector", lambda e, qc=qc: e.tensor_copy(
                        nmT[:, qc:qc + 1], t1T[:]))
                    tst = p.emit("vector", lambda e, qc=qc: e.tensor_scalar(
                        btT[:, qc:qc + 1], t1T[:], ISQ, None,
                        op0=ALU.mult), inc=sDV)
                    tok_rmx[spar] = tst
                    ta = 0
                    for kh, dslot in khs:
                        ta = p.emit("scalar",
                            lambda e, kh=kh, spar=spar, qc=qc: e.activation(
                                out=p_sb[spar][:, kh * 512:(kh + 1) * 512],
                                in_=scP[spar][:, kh * 512:(kh + 1) * 512],
                                func=AF.Exp, scale=ISQ, bias=btT[:, qc:qc + 1],
                                accum_out=laccT[:, 2 * qc + kh:2 * qc + kh + 1]),
                            waits=[(sDV, tst), (sPE, tok_ptr[spar])], inc=sAC)
                    if not first:
                        p.emit("scalar", lambda e, qc=qc: e.activation(
                            out=alT[:, qc:qc + 1], in_=t2T[:], func=AF.Exp,
                            scale=ISQ), waits=[(sDV, tst)])
                    p.emit("scalar", lambda e, qc=qc: e.activation(
                        out=acfl[:], in_=btT[:, qc:qc + 1], func=AF.Copy))
                    ta = p.emit("scalar", lambda e, qc=qc: e.activation(
                        out=acfl[:], in_=btT[:, qc:qc + 1], func=AF.Copy),
                        inc=sAC)
                    state[(i, "exp")] = ta
                    tok_exp[spar] = ta
                    state[(i, "khs")] = khs

                def pair_mid(i):
                    nonlocal tok_pevac
                    kt, qc = pairs[i]
                    qpos = qt * QT + qc * 128
                    spar = i % 2
                    vis = min(KT, qpos + 128 - kt * KT)
                    kb_max = (vis + 127) // 128
                    texp = state[(i, "exp")]
                    t = 0
                    for kb in range(kb_max):
                        t = p.emit("tensor", lambda e, kb=kb, spar=spar: e.transpose(
                            htB[:, kb * 128:(kb + 1) * 128],
                            p_sb[spar][:, kb * 128:(kb + 1) * 128], ident[:]),
                            waits=[(sAC, texp), (sDV, tok_pevac)],
                            inc=sPE if kb == kb_max - 1 else None)
                    tok_ptr[spar] = t
                    tok_pevac = p.emit("vector", lambda e, kb_max=kb_max: e.tensor_copy(
                        pf_sb[:, 0:kb_max * 128], htB[:, 0:kb_max * 128]),
                        waits=[(sPE, t)], inc=sDV)

                def pair_back(i):
                    nonlocal tok_ostt
                    kt, qc = pairs[i]
                    qpos = qt * QT + qc * 128
                    kpar = usebuf[(qt, kt)]
                    first = kt == 0
                    vis = min(KT, qpos + 128 - kt * KT)
                    kb_max = (vis + 127) // 128
                    tpv = 0
                    for dh in range(2):
                        for kb in range(kb_max):
                            tpv = p.emit(
                                "tensor",
                                lambda e, kb=kb, dh=dh, kpar=kpar: e.matmul(
                                    pvP[:, dh * 512:(dh + 1) * 512],
                                    pf_sb[:, kb * 128:(kb + 1) * 128],
                                    Vb_sb[kpar][:, kb, dh * 512:(dh + 1) * 512],
                                    start=(kb == 0), stop=(kb == kb_max - 1)),
                                waits=[(sDV, tok_pevac), (sDV, tok_ostt),
                                       ktok[kpar][1]],
                                inc=sPE if (dh == 1 and kb == kb_max - 1) else None)
                    kread[kpar] = max(kread[kpar], tpv)
                    if first:
                        tok_ostt = p.emit("vector", lambda e, qc=qc: e.tensor_copy(
                            Oq[:, qc, :], pvP[:]), waits=[(sPE, tpv)], inc=sDV)
                    else:
                        tok_ostt = p.emit("vector", lambda e, qc=qc:
                            e.scalar_tensor_tensor(
                                Oq[:, qc, :], Oq[:, qc, :], alT[:, qc:qc + 1],
                                pvP[:], op0=ALU.mult, op1=ALU.add),
                            waits=[(sPE, tpv)], inc=sDV)

                prev_kt = None
                for i in range(len(pairs)):
                    kt, qc = pairs[i]
                    new_use = kt != prev_kt
                    if new_use:
                        if (qt, kt) != uses[n_use]:
                            n_use += 1
                        assert uses[n_use] == (qt, kt)
                        prev_kt = kt
                    if i > 0:
                        pair_mid(i - 1)
                    pair_front(i)
                    if i > 0:
                        pair_back(i - 1)
                    if new_use and n_use + 2 < len(uses):
                        load_kv(n_use + 2)
                pair_mid(len(pairs) - 1)
                pair_back(len(pairs) - 1)
                emit_lupdate(len(pairs) - 1)

                for qc in range(4):
                    q0 = qt * QT
                    p.emit("vector", lambda e, qc=qc: e.reciprocal(
                        t1T[:], lT[:, qc:qc + 1]))
                    vpad()
                    te = p.emit("vector", lambda e, qc=qc, qpar=qpar:
                        e.scalar_tensor_tensor(
                            x1t[qpar][:, qc, :], Oq[:, qc, :], t1T[:],
                            x1t[qpar][:, qc, :], op0=ALU.mult, op1=ALU.add),
                        waits=[x1tok[qpar]], inc=sDV)
                    x1read[qpar] = te
                    x2st[qpar] = p.dma("gpsimd",
                        lambda e, qc=qc, q0=q0, qpar=qpar: e.dma_start(
                            out=x2d[q0 + qc * 128:q0 + (qc + 1) * 128, :],
                            in_=x1t[qpar][:, qc, :]), waits=[(sDV, te)])
            p.barrier(ALL_SEMS)

        emit_ffn_pass(x2d, out_d, with_qkv=False, tag="c_")

        p.emit("gpsimd", lambda e: e.nop(),
               waits=[(s, s.n) for v in dma_sems.values() for s in v])

        block = E(nc.Block())
        p.replay(block)
    return nc



def _host_inputs(x_c, wq, wk, wv, w13, w2, w13s, w2s, gw, gb, S):
    perm = np.concatenate([np.arange(0, D, 2), np.arange(1, D, 2)])
    f32 = np.float32
    i = np.arange(D // 2, dtype=f32)
    theta = np.power(f32(10000.0), (f32(-2.0) * (i - f32(1.0)) / f32(D))).astype(f32)
    pos = np.arange(S, dtype=f32)
    ang = (pos[:, None] * theta[None, :]).astype(f32)
    cs = np.concatenate([np.cos(ang), np.sin(ang)], axis=1).astype(f32)

    d = {
        "wqT": np.ascontiguousarray(wq[perm].T),
        "wkT": np.ascontiguousarray(wk[perm].T),
        "wvT": np.ascontiguousarray(wv.T),
        "w13T": np.ascontiguousarray(np.transpose(w13, (0, 2, 1))),
        "w2T": np.ascontiguousarray(np.transpose(w2, (0, 2, 1))),
        "w13sT": np.ascontiguousarray(w13s.T),
        "w2sT": np.ascontiguousarray(w2s.T),
        "gwT": np.ascontiguousarray(gw.T),
        "gbb": np.ascontiguousarray(np.broadcast_to(gb, (128, NE))),
        "cs": cs,
    }
    d = {k: v.astype(np.float32) for k, v in d.items()}
    d["x"] = np.ascontiguousarray(x_c, dtype=np.float32)
    return d


def kernel(L0_attn_x, L0_attn_w_q, L0_attn_w_k, L0_attn_w_v,
           L0_ffn_input_ids, L0_ffn_gate_w, L0_ffn_gate_b,
           L0_ffn_w13, L0_ffn_w2, L0_ffn_w13_s, L0_ffn_w2_s):
    x = np.asarray(L0_attn_x, dtype=np.float32)
    B, S, _ = x.shape
    wq = np.asarray(L0_attn_w_q, dtype=np.float32)
    wk = np.asarray(L0_attn_w_k, dtype=np.float32)
    wv = np.asarray(L0_attn_w_v, dtype=np.float32)
    w13 = np.asarray(L0_ffn_w13, dtype=np.float32)
    w2 = np.asarray(L0_ffn_w2, dtype=np.float32)
    w13s = np.asarray(L0_ffn_w13_s, dtype=np.float32)
    w2s = np.asarray(L0_ffn_w2_s, dtype=np.float32)
    gw = np.asarray(L0_ffn_gate_w, dtype=np.float32)
    gb = np.asarray(L0_ffn_gate_b, dtype=np.float32)

    nc = build(S)
    shared = None
    in_maps = []
    for c in range(B):
        d = _host_inputs(x[c], wq, wk, wv, w13, w2, w13s, w2s, gw, gb, S)
        if shared is None:
            shared = {k: v for k, v in d.items() if k != "x"}
        else:
            for k in shared:
                d[k] = shared[k]
        in_maps.append(d)
    res = run_bass_kernel_spmd(nc, in_maps, core_ids=list(range(B)))
    out = np.stack([res.results[c]["out"] for c in range(B)], axis=0)
    return out


# version 08
# speedup vs baseline: 1.0726x; 1.0043x over previous
import contextlib
import numpy as np
import concourse.bass as bass
import concourse.mybir as mybir
from concourse.bass_utils import run_bass_kernel_spmd

F32 = mybir.dt.float32
BF16 = mybir.dt.bfloat16
U32 = mybir.dt.uint32
AF = mybir.ActivationFunctionType
ALU = mybir.AluOpType
AX = mybir.AxisListType

D = 1024
NE = 16
NXP = 17
INTER = 512
CH = 128
GRP = 1024
QT = 512
KT = 1024
EPS = 1e-5
NEG = -1e9
RSCALE = 2.5
ISQ = 1.0 / 32.0

ENGS = ("sync", "gpsimd", "vector", "scalar", "tensor")


class Sem:
    __slots__ = ("h", "n")

    def __init__(self, h):
        self.h = h
        self.n = 0


class Prog:

    def __init__(self):
        self.items = []
        self.pending = {}
        self.dmaq = {}

    def emit(self, eng, fn, waits=(), inc=None, inc_n=1):
        w = list(self.pending.pop(eng, ()))
        for t in waits:
            if t is None:
                continue
            s, v = t
            if v > 0:
                assert v <= s.n, "wait on future token"
                w.append((s, v))
        self.items.append((eng, fn, tuple(w), (inc, inc_n) if inc is not None else None))
        if inc is not None:
            inc.n += inc_n
            return inc.n
        return None

    def set_dma_sems(self, eng, sems):
        self.dmaq[eng] = [sems, 0]

    def dma(self, eng, fn, waits=()):
        sems, cnt = self.dmaq[eng]
        s = sems[cnt % len(sems)]
        w = list(waits)
        if s.n:
            w.append((s, s.n))
        v = self.emit(eng, fn, w, inc=s, inc_n=16)
        self.dmaq[eng][1] = cnt + 1
        return (s, v)

    def barrier(self, sems):
        vals = [(s, s.n) for s in sems if s.n > 0]
        for eng in ENGS:
            self.pending[eng] = list(self.pending.get(eng, ())) + vals

    def replay(self, block):
        for name in ENGS:
            items = [(fn, w, inc) for (e, fn, w, inc) in self.items if e == name]

            def body(eng, items=items):
                last = {}
                for fn, waits, inc in items:
                    for s, v in waits:
                        if last.get(id(s), 0) < v:
                            eng.wait_ge(s.h, v)
                            last[id(s)] = v
                    ins = fn(eng)
                    if inc is not None:
                        ins.then_inc(inc[0].h, inc[1])

            getattr(block, name)(body)


def build(S, debug=False):
    grp = min(GRP, S)
    assert S % grp == 0
    gch = grp // CH
    ngrp = S // grp
    nqt = S // QT

    nc = bass.Bass("TRN2", target_bir_lowering=False)

    dt = nc.dram_tensor
    x_in = dt("x", [S, D], F32, kind="ExternalInput").ap()
    wqT_d = dt("wqT", [D, D], F32, kind="ExternalInput").ap()
    wkT_d = dt("wkT", [D, D], F32, kind="ExternalInput").ap()
    wvT_d = dt("wvT", [D, D], F32, kind="ExternalInput").ap()
    w13T = dt("w13T", [NE, D, 2 * INTER], F32, kind="ExternalInput").ap()
    w2T = dt("w2T", [NE, INTER, D], F32, kind="ExternalInput").ap()
    w13sT = dt("w13sT", [D, 2 * INTER], F32, kind="ExternalInput").ap()
    w2sT = dt("w2sT", [INTER, D], F32, kind="ExternalInput").ap()
    gwT = dt("gwT", [D, NE], F32, kind="ExternalInput").ap()
    gbb = dt("gbb", [128, NE], F32, kind="ExternalInput").ap()
    cs_in = dt("cs", [S, D], F32, kind="ExternalInput").ap()
    out_d = dt("out", [S, D], F32, kind="ExternalOutput").ap()

    ik = "ExternalOutput" if debug else "Internal"
    dbg = None
    if debug:
        dbg = {
            "u1f": dt("dbg_u1f", [128, 8, min(GRP, S)], BF16, kind="ExternalOutput").ap(),
            "scg": dt("dbg_scg", [128, min(GRP, S) // CH, NE], F32, kind="ExternalOutput").ap(),
            "comb": dt("dbg_comb", [128, min(GRP, S) // CH, NE], F32, kind="ExternalOutput").ap(),
            "yg": dt("dbg_yg", [128, min(GRP, S) // CH, D], F32, kind="ExternalOutput").ap(),
            "mscg": dt("dbg_mscg", [128, min(GRP, S) // CH, NE], F32, kind="ExternalOutput").ap(),
            "deng": dt("dbg_deng", [128, min(GRP, S) // CH], F32, kind="ExternalOutput").ap(),
            "idxg": dt("dbg_idxg", [128, min(GRP, S) // CH, 8], U32, kind="ExternalOutput").ap(),
            "s0": dt("dbg_s0", [128, INTER], BF16, kind="ExternalOutput").ap(),
            "h0": dt("dbg_h0", [128, INTER], BF16, kind="ExternalOutput").ap(),
            "hf0": dt("dbg_hf0", [128, INTER], BF16, kind="ExternalOutput").ap(),
            "y0": dt("dbg_y0", [128, D], F32, kind="ExternalOutput").ap(),
            "w13b": dt("dbg_w13b", [128, 8, D], BF16, kind="ExternalOutput").ap(),
            "gu0": dt("dbg_gu0", [128, D], F32, kind="ExternalOutput").ap(),
            "ye": dt("dbg_ye", [NXP, 128, D], F32, kind="ExternalOutput").ap(),
        }
    wb13 = dt("wb13", [NXP, D, 2 * INTER], BF16, kind="Internal").ap()
    wb2 = dt("wb2", [NXP, INTER, D], BF16, kind="Internal").ap()
    wbqkv = dt("wbqkv", [3, D, D], BF16, kind="Internal").ap()
    x1d = dt("x1d", [S, D], F32, kind=ik).ap()
    x2d = dt("x2d", [S, D], F32, kind=ik).ap()
    Qf = dt("Qf", [D, S], BF16, kind=ik).ap()
    Kf = dt("Kf", [D, S], BF16, kind=ik).ap()
    Vt_d = dt("Vt", [S, D], BF16, kind=ik).ap()

    p = Prog()
    outer = contextlib.ExitStack()
    with outer:
        E = outer.enter_context
        sPE = Sem(E(nc.semaphore("sPE")))
        sDV = Sem(E(nc.semaphore("sDV")))
        sAC = Sem(E(nc.semaphore("sAC")))
        sGI = Sem(E(nc.semaphore("sGI")))
        dma_sems = {}
        for qname, nq in (("sync", 3), ("gpsimd", 4), ("scalar", 3)):
            dma_sems[qname] = [Sem(E(nc.semaphore(f"sQ_{qname}{i}")))
                               for i in range(nq)]
            p.set_dma_sems(qname, dma_sems[qname])
        ALL_SEMS = [sPE, sDV, sAC, sGI] + [s for v in dma_sems.values()
                                           for s in v]

        ident = E(nc.sbuf_tensor("ident", [128, 128], BF16))
        identF = E(nc.sbuf_tensor("identF", [128, 128], F32))
        iota16 = E(nc.sbuf_tensor("iota16", [128, NE], F32))
        gw_sb = E(nc.sbuf_tensor("gw_sb", [128, 8, NE], F32))
        gbb_sb = E(nc.sbuf_tensor("gbb_sb", [128, NE], F32))
        epsT = E(nc.sbuf_tensor("epsT", [128, 1], F32))
        padT = E(nc.sbuf_tensor("padT", [128, 2], F32))

        def vpad():
            for _ in range(4):
                p.emit("vector", lambda e: e.memset(padT[:, 0:1], 0.0))

        t1 = p.emit("gpsimd", lambda e: e.memset(ident[:], 0.0), inc=sGI)
        t2 = p.emit("gpsimd", lambda e: e.memset(identF[:], 0.0), inc=sGI)
        p.emit("gpsimd", lambda e: e.memset(epsT[:], EPS), inc=sGI)

        p.emit("gpsimd", lambda e: e.affine_select(
            out=ident[:], in_=ident[:], compare_op=ALU.not_equal, fill=1.0,
            base=0, pattern=[[-1, 128]], channel_multiplier=1),
            waits=[(sGI, t1)], inc=sGI)
        p.emit("gpsimd", lambda e: e.affine_select(
            out=identF[:], in_=identF[:], compare_op=ALU.not_equal, fill=1.0,
            base=0, pattern=[[-1, 128]], channel_multiplier=1),
            waits=[(sGI, t2)], inc=sGI)
        p.emit("gpsimd", lambda e: e.iota(
            iota16[:], [[1, NE]], base=0, channel_multiplier=0,
            allow_small_or_imprecise_dtypes=True), inc=sGI)
        tok_const = sGI.n

        tok_gbb = p.dma("gpsimd", lambda e: e.dma_start(out=gbb_sb[:], in_=gbb))

        cast_done = []
        with contextlib.ExitStack() as st:
            stF = [st.enter_context(nc.sbuf_tensor(f"stF{i}", [128, 8, D], F32))
                   for i in range(2)]
            stB = [st.enter_context(nc.sbuf_tensor(f"stB{i}", [128, 8, D], BF16))
                   for i in range(2)]

            tok_gw = p.dma("sync", lambda e: e.dma_start(
                out=gw_sb[:],
                in_=gwT.rearrange("(dc p) e -> p dc e", p=128)))

            jobs = []
            for ee in range(NXP):
                src = (w13T[ee] if ee < NE else w13sT).rearrange(
                    "(dc p) i -> p dc i", p=128)
                jobs.append((src, wb13[ee].rearrange("(dc p) i -> p dc i", p=128), 8))
            for ee in range(NXP):
                src = (w2T[ee] if ee < NE else w2sT).rearrange(
                    "(ic p) d -> p ic d", p=128)
                jobs.append((src, wb2[ee].rearrange("(ic p) d -> p ic d", p=128), 4))
            for j, w in enumerate([wqT_d, wkT_d, wvT_d]):
                jobs.append((w.rearrange("(dc p) e -> p dc e", p=128),
                             wbqkv[j].rearrange("(dc p) e -> p dc e", p=128), 8))

            rd_tok = [0, 0]
            st_tok = [None, None]
            for ji, (src, dst, nf) in enumerate(jobs):
                b = ji % 2
                ld = p.dma("sync", lambda e, src=src, nf=nf, b=b: e.dma_start(
                    out=stF[b][:, 0:nf, :], in_=src),
                    waits=[(sDV, rd_tok[b])])
                rd_tok[b] = p.emit("vector", lambda e, nf=nf, b=b: e.tensor_copy(
                    stB[b][:, 0:nf, :], stF[b][:, 0:nf, :]),
                    waits=[ld, st_tok[b]], inc=sDV)
                st_tok[b] = p.dma("gpsimd", lambda e, dst=dst, nf=nf, b=b: e.dma_start(
                    out=dst, in_=stB[b][:, 0:nf, :]),
                    waits=[(sDV, rd_tok[b])])
                cast_done.append(st_tok[b])
        p.barrier(ALL_SEMS)

        def emit_ffn_pass(src_dram, dst_dram, with_qkv, tag, dbg=None):
            with contextlib.ExitStack() as st:
                A = st.enter_context
                xg = A(nc.sbuf_tensor(tag + "xg", [128, 2 * gch, D], F32))
                u1f = A(nc.sbuf_tensor(tag + "u1f", [128, 16, grp], BF16))
                yg = A(nc.sbuf_tensor(tag + "yg", [128, gch, D], F32))
                wbuf = A(nc.sbuf_tensor(tag + "wbuf", [128, 24576], BF16))
                s_bf = A(nc.sbuf_tensor(tag + "s_bf", [128, 2, INTER], BF16))
                h_bf = A(nc.sbuf_tensor(tag + "h_bf", [128, INTER], BF16))
                hf_bf = A(nc.sbuf_tensor(tag + "hf_bf", [128, INTER], BF16))
                comb = A(nc.sbuf_tensor(tag + "comb", [128, 2 * gch, NE], F32))
                rsg = A(nc.sbuf_tensor(tag + "rsg", [128, gch], F32))
                rqg = A(nc.sbuf_tensor(tag + "rqg", [128, gch], F32))
                mvm = A(nc.sbuf_tensor(tag + "mvm", [128, gch], F32))
                mvq = A(nc.sbuf_tensor(tag + "mvq", [128, gch], F32))
                veg = A(nc.sbuf_tensor(tag + "veg", [128, gch], F32))
                jnk = A(nc.sbuf_tensor(tag + "jnk", [128, D], F32))
                u1g32 = A(nc.sbuf_tensor(tag + "u1g32", [128, D], F32))
                lgg = A(nc.sbuf_tensor(tag + "lgg", [128, gch, NE], F32))
                scg = A(nc.sbuf_tensor(tag + "scg", [128, gch, NE], F32))
                big = A(nc.sbuf_tensor(tag + "big", [128, NE], F32))
                m8t = A(nc.sbuf_tensor(tag + "m8t", [128, 8], F32))
                idx8g = A(nc.sbuf_tensor(tag + "idx8g", [128, gch, 8], U32))
                idx8f = A(nc.sbuf_tensor(tag + "idx8f", [128, 8], F32))
                mscg = A(nc.sbuf_tensor(tag + "mscg", [128, gch, NE], F32))
                mskt = A(nc.sbuf_tensor(tag + "mskt", [128, NE], F32))
                tmp16 = A(nc.sbuf_tensor(tag + "tmp16", [128, NE], F32))
                msk2 = A(nc.sbuf_tensor(tag + "msk2", [128, NE], F32))
                msk3 = A(nc.sbuf_tensor(tag + "msk3", [128, NE], F32))
                deng = A(nc.sbuf_tensor(tag + "deng", [128, gch], F32))
                if with_qkv:
                    cs_sb = [A(nc.sbuf_tensor(tag + "cs_sb0", [128, D], F32))]
                    cs_sb = cs_sb + cs_sb
                    rt1 = jnk[:, 0:INTER]
                    rt2 = jnk[:, INTER:2 * INTER]
                    rot = A(nc.sbuf_tensor(tag + "rot", [128, D], BF16))
                    fstage = A(nc.sbuf_tensor(tag + "fstage", [128, D], BF16))
                    vstage = A(nc.sbuf_tensor(tag + "vstage", [128, D], BF16))

                guP = [A(nc.psum_tensor(tag + "guA", [128, D], F32)),
                       A(nc.psum_tensor(tag + "guB", [128, D], F32))]
                zzP = A(nc.psum_tensor(tag + "zz", [128, D], F32))
                htP = A(nc.psum_tensor(tag + "ht", [128, D], BF16))
                spP = A(nc.psum_tensor(tag + "sp", [128, 512], F32))

                def w13buf(par):
                    return wbuf[:, par * 8192:(par + 1) * 8192].rearrange(
                        "p (dc i) -> p dc i", i=D)

                def w2buf(par):
                    return wbuf[:, 16384 + par * 4096:16384 + (par + 1) * 4096] \
                        .rearrange("p (ic d) -> p ic d", d=D)

                wq_b = wbuf[:, 0:8192].rearrange("p (dc e) -> p dc e", e=D)
                wk_b = wbuf[:, 8192:16384].rearrange("p (dc e) -> p dc e", e=D)
                wv_b = wbuf[:, 16384:24576].rearrange("p (dc e) -> p dc e", e=D)

                last_read = {"w13_0": 0, "w13_1": 0, "w2_0": 0, "w2_1": 0}
                wtok = {}
                xg_freed = [None, None]
                u1f_freed = [0, 0]
                sp_free = [0]
                ht_free = [0]
                gu_guard = [0, 0]
                sil_guard = [0, 0]
                zz_guard = [0]
                pre_tok = {}

                def load_weights(g, e):
                    par = e % 2
                    t1 = p.dma("sync", lambda eng, e=e, par=par: eng.dma_start(
                        out=w13buf(par),
                        in_=wb13[e].rearrange("(dc p) i -> p dc i", p=128)),
                        waits=[(sPE, last_read[f"w13_{par}"]),
                               cast_done[e]])
                    t2 = p.dma("sync", lambda eng, e=e, par=par: eng.dma_start(
                        out=w2buf(par),
                        in_=wb2[e].rearrange("(ic p) d -> p ic d", p=128)),
                        waits=[(sPE, last_read[f"w2_{par}"]),
                               cast_done[NXP + e]])
                    wtok[(g, e)] = (t1, t2)

                def load_qkv(g):
                    toks = []
                    for j in range(3):
                        dst = [wq_b, wk_b, wv_b][j]
                        regions = [["w13_0"], ["w13_1"], ["w2_0", "w2_1"]][j]
                        t = p.dma("sync", lambda eng, j=j, dst=dst: eng.dma_start(
                            out=dst,
                            in_=wbqkv[j].rearrange("(dc p) e -> p dc e", p=128)),
                            waits=[(sPE, max(last_read[r] for r in regions)),
                                   cast_done[2 * NXP + j]])
                        toks.append(t)
                    wtok[("qkv", g)] = toks

                def emit_stats(xoff, xg_ld):
                    for c in range(gch):
                        p.emit("vector", lambda e, c=c: e.reduce_sum(
                            rsg[:, c:c + 1], xg[:, xoff + c, :], axis=AX.X),
                            waits=[xg_ld])
                    for c in range(gch):
                        p.emit("vector", lambda e, c=c: e.scalar_tensor_tensor(
                            jnk[:], xg[:, xoff + c, :], 1.0, xg[:, xoff + c, :],
                            op0=ALU.mult, op1=ALU.mult,
                            accum_out=rqg[:, c:c + 1]))
                    for _ in range(3):
                        p.emit("vector", lambda e: e.memset(padT[:, 0:1], 0.0))
                    tdv = p.emit("vector", lambda e: e.memset(padT[:, 1:2], 0.0),
                                 inc=sDV)
                    p.emit("scalar", lambda e: e.activation(
                        out=mvm[:], in_=rsg[:], func=AF.Copy, scale=1.0 / D),
                        waits=[(sDV, tdv)])
                    tac = p.emit("scalar", lambda e: e.activation(
                        out=mvq[:], in_=rqg[:], func=AF.Copy, scale=1.0 / D),
                        inc=sAC)
                    p.emit("vector", lambda e: e.tensor_tensor(
                        rsg[:], mvm[:], mvm[:], op=ALU.mult),
                        waits=[(sAC, tac)])
                    vpad()
                    tdv2 = p.emit("vector", lambda e: e.tensor_tensor(
                        veg[:], mvq[:], rsg[:], op=ALU.subtract), inc=sDV)
                    tac2 = p.emit("scalar", lambda e: e.activation(
                        out=veg[:], in_=veg[:], func=AF.Sqrt, bias=epsT[:]),
                        waits=[(sDV, tdv2)], inc=sAC)
                    p.emit("vector", lambda e: e.reciprocal(veg[:], veg[:]),
                           waits=[(sAC, tac2)])
                    vpad()

                def emit_ln_chunk(xoff, uoff, c, gate, piecewise=False):
                    tn = p.emit("vector", lambda e, c=c: e.tensor_scalar(
                        jnk[:], xg[:, xoff + c, :], mvm[:, c:c + 1],
                        veg[:, c:c + 1], op0=ALU.subtract, op1=ALU.mult),
                        inc=sDV)
                    if piecewise:
                        yield
                        yield
                    ev = 0
                    for rr in range(2):
                        tt = 0
                        for b in range(4):
                            dc = rr * 4 + b
                            tt = p.emit("tensor", lambda e, dc=dc, b=b:
                                e.transpose(
                                    spP[:, b * 128:(b + 1) * 128],
                                    jnk[:, dc * 128:(dc + 1) * 128],
                                    identF[:]),
                                waits=[(sDV, tn), (sDV, sp_free[0]),
                                       (sGI, tok_const)],
                                inc=sPE if b == 3 else None)
                        ev = p.emit("vector", lambda e, c=c, rr=rr:
                            e.tensor_copy(
                                u1f[:, uoff + rr * 4:uoff + (rr + 1) * 4,
                                    c * 128:(c + 1) * 128],
                                spP[:].rearrange("p (b t) -> p b t", t=128)),
                            waits=[(sPE, tt)], inc=sDV)
                        sp_free[0] = ev
                        if gate:
                            ev = p.emit("vector", lambda e, rr=rr:
                                e.tensor_copy(
                                    u1g32[:, rr * 512:(rr + 1) * 512],
                                    spP[:]),
                                inc=sDV)
                            sp_free[0] = ev
                    if piecewise:
                        yield
                    if gate:
                        tg = 0
                        for dc in range(8):
                            tg = p.emit("tensor", lambda e, dc=dc: e.matmul(
                                spP[:, 0:NE],
                                u1g32[:, dc * 128:(dc + 1) * 128],
                                gw_sb[:, dc, :],
                                start=(dc == 0), stop=(dc == 7)),
                                waits=[(sDV, ev), tok_gw],
                                inc=sPE if dc == 7 else None)
                        ev = p.emit("vector", lambda e, c=c: e.tensor_copy(
                            lgg[:, c, :], spP[:, 0:NE]),
                            waits=[(sPE, tg)], inc=sDV)
                        sp_free[0] = ev
                    yield ev

                def emit_routing(coff):
                    tl = sDV.n
                    tsg = p.emit("scalar", lambda e: e.activation(
                        out=scg[:].rearrange("p c e -> p (c e)"),
                        in_=lgg[:].rearrange("p c e -> p (c e)"),
                        func=AF.Sigmoid), waits=[(sDV, tl)], inc=sAC)
                    for c in range(gch):
                        p.emit("vector", lambda e, c=c: e.tensor_tensor(
                            big[:], scg[:, c, :], gbb_sb[:], op=ALU.add),
                            waits=[(sAC, tsg), tok_gbb])
                        vpad()
                        p.emit("vector", lambda e: e.max(m8t[:], big[:]))
                        vpad()
                        p.emit("vector", lambda e, c=c: e.max_index(
                            idx8g[:, c, :], m8t[:], big[:]))
                    for c in range(gch):
                        p.emit("vector", lambda e, c=c: e.tensor_copy(
                            idx8f[:], idx8g[:, c, :]))
                        vpad()
                        for j in range(4):
                            dst = [mskt, tmp16, msk2, msk3][j]
                            p.emit("vector", lambda e, j=j, dst=dst: e.tensor_scalar(
                                dst[:], iota16[:], idx8f[:, j:j + 1], None,
                                op0=ALU.is_equal), waits=[(sGI, tok_const)])
                        vpad()
                        p.emit("vector", lambda e: e.tensor_tensor(
                            mskt[:], mskt[:], tmp16[:], op=ALU.add))
                        p.emit("vector", lambda e: e.tensor_tensor(
                            msk2[:], msk2[:], msk3[:], op=ALU.add))
                        vpad()
                        p.emit("vector", lambda e: e.tensor_tensor(
                            mskt[:], mskt[:], msk2[:], op=ALU.add))
                        vpad()
                        p.emit("vector", lambda e, c=c: e.tensor_tensor(
                            mscg[:, c, :], mskt[:], scg[:, c, :], op=ALU.mult))
                        vpad()
                        p.emit("vector", lambda e, c=c: e.reduce_sum(
                            deng[:, c:c + 1], mscg[:, c, :], axis=AX.X))
                    vpad()
                    vpad()
                    p.emit("vector", lambda e: e.reciprocal(deng[:], deng[:]))
                    vpad()
                    t = 0
                    for c in range(gch):
                        t = p.emit("vector", lambda e, c=c, coff=coff: e.tensor_scalar(
                            comb[:, coff + c, :], mscg[:, c, :],
                            deng[:, c:c + 1], RSCALE,
                            op0=ALU.mult, op1=ALU.mult), inc=sDV)
                    return t

                def prelude(g):
                    par = g % 2
                    xoff = par * gch
                    xg_ld = p.dma("gpsimd", lambda e, g=g, xoff=xoff: e.dma_start(
                        out=xg[:, xoff:xoff + gch, :],
                        in_=src_dram[g * grp:(g + 1) * grp, :].rearrange(
                            "(c p) d -> p c d", p=128)),
                        waits=[xg_freed[par]])
                    yield
                    emit_stats(xoff, xg_ld)
                    yield
                    ev = 0
                    for c in range(gch):
                        for r in emit_ln_chunk(xoff, par * 8, c, gate=True,
                                               piecewise=True):
                            if r is not None:
                                ev = r
                            yield
                    ctok = emit_routing(par * gch)
                    pre_tok[g] = {"ev": ev, "comb": ctok, "xg_ld": xg_ld}
                    yield

                def run_prelude_to_end(gen):
                    for _ in gen:
                        pass

                load_weights(0, 0)
                load_weights(0, 1)
                run_prelude_to_end(prelude(0))

                for g in range(ngrp):
                    g0 = g * grp
                    par = g % 2
                    xoff = par * gch
                    ev = pre_tok[g]["ev"]
                    xg_ld = pre_tok[g]["xg_ld"]


                    units = [(e, c) for e in range(NXP) for c in range(gch)]
                    tok_gu = {}
                    tok_tr = {}
                    tok_z = {}
                    tok_h = {-2: gu_guard[0], -1: gu_guard[1]}
                    tok_ev = {-1: ht_free[0]}
                    tok_sil = {-2: sil_guard[0], -1: sil_guard[1]}
                    tok_y = {-1: zz_guard[0]}

                    def pe_trans(u):
                        t = 0
                        for ib in range(4):
                            t = p.emit("tensor", lambda e, ib=ib: e.transpose(
                                htP[:, ib * 128:(ib + 1) * 128],
                                h_bf[:, ib * 128:(ib + 1) * 128], ident[:]),
                                waits=[(sDV, tok_h[u]), (sDV, tok_ev[u - 1])],
                                inc=sPE if ib == 3 else None)
                        tok_tr[u] = t

                    def dv_evac(u):
                        tok_ev[u] = p.emit("vector", lambda e: e.tensor_copy(
                            hf_bf[:], htP[:, 0:512]),
                            waits=[(sPE, tok_tr[u])], inc=sDV)

                    def pe_gu(u):
                        eu, cu = units[u]
                        bpar = u % 2
                        w13t = wtok[(g, eu)][0]
                        t = 0
                        for ih in range(2):
                            for dc in range(8):
                                t = p.emit(
                                    "tensor",
                                    lambda e, cu=cu, dc=dc, ih=ih, bpar=bpar,
                                           eu=eu, xoff=xoff, par=par:
                                    e.matmul(
                                        guP[bpar][:, ih * 512:(ih + 1) * 512],
                                        u1f[:, par * 8 + dc,
                                            cu * 128:(cu + 1) * 128],
                                        w13buf(eu % 2)[:, dc,
                                                       ih * 512:(ih + 1) * 512],
                                        start=(dc == 0), stop=(dc == 7)),
                                    waits=[w13t,
                                           (sDV, tok_h[u - 2]),
                                           (sAC, tok_sil[u - 2]),
                                           (sDV, ev)],
                                    inc=sPE if (ih == 1 and dc == 7) else None)
                        tok_gu[u] = t
                        last_read[f"w13_{eu % 2}"] = t
                        u1f_freed[par] = t

                    def pe_z(u):
                        eu, cu = units[u]
                        w2t = wtok[(g, eu)][1]
                        t = 0
                        for dh in range(2):
                            for ic in range(4):
                                t = p.emit(
                                    "tensor",
                                    lambda e, ic=ic, dh=dh, eu=eu: e.matmul(
                                        zzP[:, dh * 512:(dh + 1) * 512],
                                        hf_bf[:, ic * 128:(ic + 1) * 128],
                                        w2buf(eu % 2)[:, ic,
                                                      dh * 512:(dh + 1) * 512],
                                        start=(ic == 0), stop=(ic == 3)),
                                    waits=[w2t, (sDV, tok_ev[u]),
                                           (sDV, tok_y[u - 1])],
                                    inc=sPE if (dh == 1 and ic == 3) else None)
                        tok_z[u] = t
                        last_read[f"w2_{eu % 2}"] = t

                    def ac_silu(u):
                        bpar = u % 2
                        tok_sil[u] = p.emit("scalar", lambda e, bpar=bpar:
                            e.activation(
                                out=s_bf[:, bpar, :], in_=guP[bpar][:, 0:512],
                                func=AF.Silu),
                            waits=[(sPE, tok_gu[u]), (sDV, tok_h[u - 2])],
                            inc=sAC)

                    def dv_h(u):
                        bpar = u % 2
                        tok_h[u] = p.emit("vector", lambda e, bpar=bpar:
                            e.tensor_tensor(
                                h_bf[:], guP[bpar][:, 512:1024],
                                s_bf[:, bpar, :], op=ALU.mult),
                            waits=[(sAC, tok_sil[u]), (sPE, tok_gu[u])], inc=sDV)

                    def dv_y(u):
                        eu, cu = units[u]
                        coff = par * gch
                        if eu == 0:
                            tok_y[u] = p.emit("vector", lambda e, cu=cu, eu=eu,
                                coff=coff: e.tensor_scalar(
                                    yg[:, cu, :], zzP[:],
                                    comb[:, coff + cu, eu:eu + 1],
                                    None, op0=ALU.mult),
                                waits=[(sPE, tok_z[u])], inc=sDV)
                        elif eu < NE:
                            tok_y[u] = p.emit("vector", lambda e, cu=cu, eu=eu,
                                coff=coff: e.scalar_tensor_tensor(
                                    yg[:, cu, :], zzP[:],
                                    comb[:, coff + cu, eu:eu + 1],
                                    yg[:, cu, :], op0=ALU.mult, op1=ALU.add),
                                waits=[(sPE, tok_z[u])], inc=sDV)
                        else:
                            tok_y[u] = p.emit("vector", lambda e, cu=cu:
                                e.scalar_tensor_tensor(
                                    yg[:, cu, :], zzP[:], 1.0,
                                    yg[:, cu, :], op0=ALU.mult, op1=ALU.add),
                                waits=[(sPE, tok_z[u])], inc=sDV)

                    ngen = prelude(g + 1) if g + 1 < ngrp else None
                    gen_done = ngen is None
                    step = max(1, len(units) // 30)

                    for u in range(len(units)):
                        eu, cu = units[u]
                        if u > 0:
                            pe_trans(u - 1)
                            dv_evac(u - 1)
                        pe_gu(u)
                        if u > 0:
                            pe_z(u - 1)
                        if cu == 0:
                            if eu + 1 < NXP and (g, eu + 1) not in wtok:
                                load_weights(g, eu + 1)
                            elif (eu == NXP - 1 and not with_qkv
                                  and g + 1 < ngrp):
                                load_weights(g + 1, 1)
                        ac_silu(u)
                        dv_h(u)
                        if u > 0:
                            dv_y(u - 1)
                        if (not gen_done and u >= 8 and (u - 8) % step == 0):
                            try:
                                next(ngen)
                            except StopIteration:
                                gen_done = True
                    u_last = len(units) - 1
                    pe_trans(u_last)
                    dv_evac(u_last)
                    pe_z(u_last)
                    dv_y(u_last)
                    if not gen_done:
                        run_prelude_to_end(ngen)
                    if with_qkv:
                        load_qkv(g)
                    elif g + 1 < ngrp:
                        load_weights(g + 1, 0)

                    ep = 0
                    for c in range(gch):
                        ep = p.emit("vector", lambda e, c=c, xoff=xoff:
                            e.scalar_tensor_tensor(
                                xg[:, xoff + c, :], yg[:, c, :], 0.5,
                                xg[:, xoff + c, :],
                                op0=ALU.mult, op1=ALU.add),
                            waits=[(sDV, tok_y[(NXP - 1) * gch + c])], inc=sDV)

                    if not with_qkv:
                        xg_freed[par] = p.dma("gpsimd",
                            lambda e, g0=g0, xoff=xoff: e.dma_start(
                                out=dst_dram[g0:g0 + grp, :].rearrange(
                                    "(c p) d -> p c d", p=128),
                                in_=xg[:, xoff:xoff + gch, :]),
                            waits=[(sDV, ep)])
                        u1f_freed[par] = max(u1f_freed[par], tok_gu[u_last])
                        gu_guard[0] = tok_h[u_last - 1]
                        gu_guard[1] = tok_h[u_last]
                        sil_guard[0] = tok_sil[u_last - 1]
                        sil_guard[1] = tok_sil[u_last]
                        zz_guard[0] = tok_y[u_last]
                        continue

                    emit_stats(xoff, None)
                    ev2 = 0
                    for c in range(gch):
                        for r in emit_ln_chunk(xoff, par * 8, c, gate=False):
                            if r is not None:
                                ev2 = r
                    qkvt = wtok[("qkv", g)]
                    ac_g = sAC.n
                    rope_done = [0, 0]
                    psum_free = [0, 0, 0]
                    ht_guard = tok_ev[u_last]
                    fst_store = None
                    vst_store = None
                    for c in range(gch):
                        cld = p.dma("scalar", lambda e, g0=g0, c=c:
                            e.dma_start(
                                out=cs_sb[0][:],
                                in_=cs_in[g0 + c * 128:g0 + (c + 1) * 128, :]),
                            waits=[(sDV, max(rope_done))])
                        mt = [0, 0, 0]
                        for j in range(3):
                            psum = [guP[0], guP[1], zzP][j]
                            wb = [wq_b, wk_b, wv_b][j]
                            wt = qkvt[j]
                            t = 0
                            for eh in range(2):
                                for dc in range(8):
                                    t = p.emit(
                                        "tensor",
                                        lambda e, c=c, dc=dc, eh=eh, psum=psum,
                                               wb=wb, par=par: e.matmul(
                                            psum[:, eh * 512:(eh + 1) * 512],
                                            u1f[:, par * 8 + dc,
                                                c * 128:(c + 1) * 128],
                                            wb[:, dc, eh * 512:(eh + 1) * 512],
                                            start=(dc == 0), stop=(dc == 7)),
                                        waits=[wt, (sDV, ev2),
                                               (sDV, psum_free[j]), (sAC, ac_g)],
                                        inc=sPE if (eh == 1 and dc == 7) else None)
                            mt[j] = t
                            u1f_freed[par] = t
                        rtoks = [0, 0]
                        tts = [0, 0]
                        for j in range(2):
                            psum = [guP[0], guP[1]][j]
                            p.emit("vector", lambda e, psum=psum, j=j:
                                e.tensor_tensor(
                                    rt1, psum[:, 0:512],
                                    cs_sb[0][:, 0:512], op=ALU.mult),
                                waits=[(sPE, mt[j]), cld,
                                       (sPE, tts[0] if j == 1 else 0)])
                            p.emit("vector", lambda e, psum=psum:
                                e.tensor_tensor(
                                    rt2, psum[:, 512:1024],
                                    cs_sb[0][:, 512:1024], op=ALU.mult))
                            p.emit("vector", lambda e: e.tensor_tensor(
                                rot[:, 0:512], rt1, rt2, op=ALU.add))
                            p.emit("vector", lambda e, psum=psum:
                                e.tensor_tensor(
                                    rt1, psum[:, 512:1024],
                                    cs_sb[0][:, 0:512], op=ALU.mult))
                            p.emit("vector", lambda e, psum=psum:
                                e.tensor_tensor(
                                    rt2, psum[:, 0:512],
                                    cs_sb[0][:, 512:1024], op=ALU.mult))
                            pq = p.emit("vector", lambda e: e.tensor_tensor(
                                rot[:, 512:1024], rt1, rt2,
                                op=ALU.subtract), inc=sDV)
                            rtoks[j] = pq
                            psum_free[j] = pq
                            tt = 0
                            for dc in range(8):
                                tt = p.emit("tensor", lambda e, dc=dc:
                                    e.transpose(
                                        htP[:, dc * 128:(dc + 1) * 128],
                                        rot[:, dc * 128:(dc + 1) * 128],
                                        ident[:]),
                                    waits=[(sDV, pq), (sDV, ht_guard)],
                                    inc=sPE if dc == 7 else None)
                            tts[j] = tt
                            fe = p.emit("vector", lambda e: e.tensor_copy(
                                fstage[:], htP[:]),
                                waits=[(sPE, tt), fst_store], inc=sDV)
                            ht_guard = fe
                            dstf = Qf if j == 0 else Kf
                            fst_store = p.dma("gpsimd",
                                lambda e, dstf=dstf, g0=g0, c=c: e.dma_start(
                                    out=dstf.rearrange(
                                        "(dc p) q -> p dc q", p=128)[
                                        :, :, g0 + c * 128:g0 + (c + 1) * 128],
                                    in_=fstage[:].rearrange(
                                        "p (dc t) -> p dc t", t=128)),
                                waits=[(sDV, fe)])
                            if j == 0:
                                pv = p.emit("vector", lambda e: e.tensor_copy(
                                    vstage[:], zzP[:]),
                                    waits=[(sPE, mt[2]), vst_store], inc=sDV)
                                psum_free[2] = pv
                                vst_store = p.dma("gpsimd",
                                    lambda e, g0=g0, c=c: e.dma_start(
                                        out=Vt_d[g0 + c * 128:
                                                 g0 + (c + 1) * 128, :],
                                        in_=vstage[:]),
                                    waits=[(sDV, pv)])
                        rope_done = [rtoks[0], rtoks[1]]
                    ht_free[0] = ht_guard
                    gu_guard[0] = psum_free[0]
                    gu_guard[1] = psum_free[1]
                    sil_guard[0] = sil_guard[1] = sAC.n
                    zz_guard[0] = psum_free[2]
                    last_read["w13_0"] = sPE.n
                    last_read["w13_1"] = sPE.n
                    last_read["w2_0"] = sPE.n
                    last_read["w2_1"] = sPE.n
                    if g + 1 < ngrp:
                        load_weights(g + 1, 0)
                        load_weights(g + 1, 1)
                    xg_freed[par] = p.dma("gpsimd",
                        lambda e, g0=g0, xoff=xoff: e.dma_start(
                            out=dst_dram[g0:g0 + grp, :].rearrange(
                                "(c p) d -> p c d", p=128),
                            in_=xg[:, xoff:xoff + gch, :]),
                        waits=[(sDV, ep)])
            p.barrier(ALL_SEMS)

        emit_ffn_pass(x_in, x1d, with_qkv=True, tag="a_", dbg=dbg)

        with contextlib.ExitStack() as st:
            A = st.enter_context
            msk4 = A(nc.sbuf_tensor("msk4", [128, 4, 512], F32))
            t4 = p.emit("gpsimd", lambda e: e.memset(msk4[:], 0.0), inc=sGI)
            for k in range(4):
                p.emit("gpsimd", lambda e, k=k: e.affine_select(
                    out=msk4[:, k, :], in_=msk4[:, k, :], compare_op=ALU.is_ge,
                    fill=NEG, base=(k * 128), pattern=[[-1, 512]],
                    channel_multiplier=1), waits=[(sGI, t4)], inc=sGI)
            tok_msk = sGI.n
            Qt_sb = [A(nc.sbuf_tensor(f"Qt{i}", [128, 8, QT], BF16)) for i in range(2)]
            Kt_sb = [A(nc.sbuf_tensor(f"Kt{i}", [128, 8, KT], BF16)) for i in range(3)]
            Vb_sb = [A(nc.sbuf_tensor(f"Vb{i}", [128, 8, D], BF16)) for i in range(3)]
            Oq = A(nc.sbuf_tensor("Oq", [128, 4, D], F32))
            x1t = [A(nc.sbuf_tensor(f"x1t{i}", [128, 4, D], F32)) for i in range(2)]
            p_sb = [A(nc.sbuf_tensor(f"p_sb{i}", [128, KT], BF16)) for i in range(2)]
            pf_sb = A(nc.sbuf_tensor("pf_sb", [128, KT], BF16))
            nmT = A(nc.sbuf_tensor("nmT", [128, 4], F32))
            lT = A(nc.sbuf_tensor("lT", [128, 4], F32))
            alT = A(nc.sbuf_tensor("alT", [128, 4], F32))
            btT = A(nc.sbuf_tensor("btT", [128, 4], F32))
            laccT = A(nc.sbuf_tensor("laccT", [128, 8], F32))
            rmxT = A(nc.sbuf_tensor("rmxT", [128, 2], F32))
            t1T = A(nc.sbuf_tensor("t1T", [128, 1], F32))
            acfl = A(nc.sbuf_tensor("acfl", [128, 1], F32))
            t2T = A(nc.sbuf_tensor("t2T", [128, 1], F32))
            lsT = A(nc.sbuf_tensor("lsT", [128, 1], F32))

            scP = [A(nc.psum_tensor(f"sc{i}", [128, KT], F32)) for i in range(2)]
            pvP = A(nc.psum_tensor("pv", [128, D], F32))
            htB = A(nc.psum_tensor("htB", [128, KT], BF16))

            ktok = [None, None, None]
            kread = [0, 0, 0]
            qtok = [None, None]
            qread = [0, 0]
            x1tok = [None, None]
            x1read = [0, 0]
            x2st = [None, None]

            uses = []
            usebuf = {}
            for _qt in range(nqt):
                _nkt = (_qt * QT + QT + KT - 1) // KT
                for _kt in range(_nkt):
                    usebuf[(_qt, _kt)] = len(uses) % 3
                    uses.append((_qt, _kt))

            def load_kv(n):
                _, kt = uses[n]
                buf = n % 3
                k0 = kt * KT
                tka = p.dma("sync", lambda e, k0=k0, buf=buf: e.dma_start(
                    out=Kt_sb[buf][:],
                    in_=Kf.rearrange("(dc p) k -> p dc k", p=128)[:, :, k0:k0 + KT]),
                    waits=[(sPE, kread[buf])])
                tkb = p.dma("sync", lambda e, k0=k0, buf=buf: e.dma_start(
                    out=Vb_sb[buf][:],
                    in_=Vt_d[k0:k0 + KT, :].rearrange("(kc p) d -> p kc d", p=128)),
                    waits=[(sPE, kread[buf])])
                ktok[buf] = (tka, tkb)

            def load_q(qt):
                par = qt % 2
                q0 = qt * QT
                qtok[par] = p.dma("scalar", lambda e, q0=q0, par=par: e.dma_start(
                    out=Qt_sb[par][:],
                    in_=Qf.rearrange("(dc p) q -> p dc q", p=128)[:, :, q0:q0 + QT]),
                    waits=[(sPE, qread[par])])
                x1tok[par] = p.dma("scalar", lambda e, q0=q0, par=par: e.dma_start(
                    out=x1t[par][:],
                    in_=x1d[q0:q0 + QT, :].rearrange("(qc p) d -> p qc d", p=128)),
                    waits=[(sDV, x1read[par]), x2st[par]])

            load_q(0)
            load_kv(0)
            if len(uses) > 1:
                load_kv(1)
            n_use = 0

            tok_exp = [0, 0]
            tok_rmx = [0, 0]
            tok_ptr = [0, 0]
            tok_pevac = 0
            tok_ostt = 0

            for qt in range(nqt):
                nkt = (qt * QT + QT + KT - 1) // KT
                qpar = qt % 2
                if qt + 1 < nqt:
                    load_q(qt + 1)

                pairs = []
                for kt in range(nkt):
                    for qc in range(4):
                        qpos = qt * QT + qc * 128
                        if kt * KT > qpos + 127:
                            continue
                        pairs.append((kt, qc))

                state = {}

                def emit_lupdate(j):
                    ktj, qcj = pairs[j]
                    khsj = state[(j, "khs")]
                    taj = state[(j, "exp")]
                    firstj = ktj == 0
                    if len(khsj) == 2:
                        p.emit("vector", lambda e, qcj=qcj: e.tensor_tensor(
                            lsT[:], laccT[:, 2 * qcj:2 * qcj + 1],
                            laccT[:, 2 * qcj + 1:2 * qcj + 2], op=ALU.add),
                            waits=[(sAC, taj)])
                    else:
                        kh0 = khsj[0][0]
                        p.emit("vector", lambda e, qcj=qcj, kh0=kh0: e.tensor_copy(
                            lsT[:], laccT[:, 2 * qcj + kh0:2 * qcj + kh0 + 1]),
                            waits=[(sAC, taj)])
                    vpad()
                    if firstj:
                        p.emit("vector", lambda e, qcj=qcj: e.tensor_copy(
                            lT[:, qcj:qcj + 1], lsT[:]))
                    else:
                        p.emit("vector", lambda e, qcj=qcj: e.scalar_tensor_tensor(
                            lT[:, qcj:qcj + 1], lT[:, qcj:qcj + 1],
                            alT[:, qcj:qcj + 1], lsT[:],
                            op0=ALU.mult, op1=ALU.add))

                def pair_front(i):
                    kt, qc = pairs[i]
                    qpos = qt * QT + qc * 128
                    spar = i % 2
                    kpar = usebuf[(qt, kt)]
                    khs = []
                    for kh in range(2):
                        kp = kt * KT + kh * 512
                        if kp > qpos + 127:
                            continue
                        delta = qpos - kp
                        khs.append((kh, None if delta >= 512 else delta // 128))
                    t = 0
                    for kh, dslot in khs:
                        for dc in range(8):
                            t = p.emit(
                                "tensor",
                                lambda e, kh=kh, dc=dc, spar=spar, kpar=kpar,
                                       qpar=qpar, qc=qc: e.matmul(
                                    scP[spar][:, kh * 512:(kh + 1) * 512],
                                    Qt_sb[qpar][:, dc, qc * 128:(qc + 1) * 128],
                                    Kt_sb[kpar][:, dc, kh * 512:(kh + 1) * 512],
                                    start=(dc == 0), stop=(dc == 7)),
                                waits=[qtok[qpar], ktok[kpar][0], ktok[kpar][1],
                                       (sAC, tok_exp[spar]), (sDV, tok_rmx[spar])],
                                inc=sPE if (dc == 7 and kh == khs[-1][0]) else None)
                    state[(i, "sc")] = t
                    kread[kpar] = max(kread[kpar], t)
                    qread[qpar] = max(qread[qpar], t)
                    first = kt == 0
                    for kh, dslot in khs:
                        if dslot is not None:
                            p.emit("vector", lambda e, kh=kh, spar=spar, dslot=dslot:
                                e.tensor_tensor(
                                    scP[spar][:, kh * 512:(kh + 1) * 512],
                                    scP[spar][:, kh * 512:(kh + 1) * 512],
                                    msk4[:, dslot, :], op=ALU.add),
                                waits=[(sPE, t), (sGI, tok_msk)])
                        p.emit("vector", lambda e, kh=kh, spar=spar: e.reduce_max(
                            rmxT[:, kh:kh + 1],
                            scP[spar][:, kh * 512:(kh + 1) * 512],
                            axis=AX.X, negate=True), waits=[(sPE, t)])
                    if i > 0:
                        emit_lupdate(i - 1)
                    else:
                        vpad()
                    if len(khs) == 2:
                        p.emit("vector", lambda e: e.tensor_tensor(
                            t1T[:], rmxT[:, 0:1], rmxT[:, 1:2], op=ALU.min))
                    else:
                        kh0 = khs[0][0]
                        p.emit("vector", lambda e, kh0=kh0: e.tensor_copy(
                            t1T[:], rmxT[:, kh0:kh0 + 1]))
                    vpad()
                    if not first:
                        p.emit("vector", lambda e, qc=qc: e.tensor_tensor(
                            t1T[:], t1T[:], nmT[:, qc:qc + 1], op=ALU.min))
                        vpad()
                        p.emit("vector", lambda e, qc=qc: e.tensor_tensor(
                            t2T[:], t1T[:], nmT[:, qc:qc + 1], op=ALU.subtract))
                    p.emit("vector", lambda e, qc=qc: e.tensor_copy(
                        nmT[:, qc:qc + 1], t1T[:]))
                    tst = p.emit("vector", lambda e, qc=qc: e.tensor_scalar(
                        btT[:, qc:qc + 1], t1T[:], ISQ, None,
                        op0=ALU.mult), inc=sDV)
                    tok_rmx[spar] = tst
                    ta = 0
                    for kh, dslot in khs:
                        ta = p.emit("scalar",
                            lambda e, kh=kh, spar=spar, qc=qc: e.activation(
                                out=p_sb[spar][:, kh * 512:(kh + 1) * 512],
                                in_=scP[spar][:, kh * 512:(kh + 1) * 512],
                                func=AF.Exp, scale=ISQ, bias=btT[:, qc:qc + 1],
                                accum_out=laccT[:, 2 * qc + kh:2 * qc + kh + 1]),
                            waits=[(sDV, tst), (sPE, tok_ptr[spar])], inc=sAC)
                    if not first:
                        p.emit("scalar", lambda e, qc=qc: e.activation(
                            out=alT[:, qc:qc + 1], in_=t2T[:], func=AF.Exp,
                            scale=ISQ), waits=[(sDV, tst)])
                    p.emit("scalar", lambda e, qc=qc: e.activation(
                        out=acfl[:], in_=btT[:, qc:qc + 1], func=AF.Copy))
                    ta = p.emit("scalar", lambda e, qc=qc: e.activation(
                        out=acfl[:], in_=btT[:, qc:qc + 1], func=AF.Copy),
                        inc=sAC)
                    state[(i, "exp")] = ta
                    tok_exp[spar] = ta
                    state[(i, "khs")] = khs

                def pair_mid(i):
                    nonlocal tok_pevac
                    kt, qc = pairs[i]
                    qpos = qt * QT + qc * 128
                    spar = i % 2
                    vis = min(KT, qpos + 128 - kt * KT)
                    kb_max = (vis + 127) // 128
                    texp = state[(i, "exp")]
                    t = 0
                    for kb in range(kb_max):
                        t = p.emit("tensor", lambda e, kb=kb, spar=spar: e.transpose(
                            htB[:, kb * 128:(kb + 1) * 128],
                            p_sb[spar][:, kb * 128:(kb + 1) * 128], ident[:]),
                            waits=[(sAC, texp), (sDV, tok_pevac)],
                            inc=sPE if kb == kb_max - 1 else None)
                    tok_ptr[spar] = t
                    tok_pevac = p.emit("vector", lambda e, kb_max=kb_max: e.tensor_copy(
                        pf_sb[:, 0:kb_max * 128], htB[:, 0:kb_max * 128]),
                        waits=[(sPE, t)], inc=sDV)

                def pair_back(i):
                    nonlocal tok_ostt
                    kt, qc = pairs[i]
                    qpos = qt * QT + qc * 128
                    kpar = usebuf[(qt, kt)]
                    first = kt == 0
                    vis = min(KT, qpos + 128 - kt * KT)
                    kb_max = (vis + 127) // 128
                    tpv = 0
                    for dh in range(2):
                        for kb in range(kb_max):
                            tpv = p.emit(
                                "tensor",
                                lambda e, kb=kb, dh=dh, kpar=kpar: e.matmul(
                                    pvP[:, dh * 512:(dh + 1) * 512],
                                    pf_sb[:, kb * 128:(kb + 1) * 128],
                                    Vb_sb[kpar][:, kb, dh * 512:(dh + 1) * 512],
                                    start=(kb == 0), stop=(kb == kb_max - 1)),
                                waits=[(sDV, tok_pevac), (sDV, tok_ostt),
                                       ktok[kpar][1]],
                                inc=sPE if (dh == 1 and kb == kb_max - 1) else None)
                    kread[kpar] = max(kread[kpar], tpv)
                    if first:
                        tok_ostt = p.emit("vector", lambda e, qc=qc: e.tensor_copy(
                            Oq[:, qc, :], pvP[:]), waits=[(sPE, tpv)], inc=sDV)
                    else:
                        tok_ostt = p.emit("vector", lambda e, qc=qc:
                            e.scalar_tensor_tensor(
                                Oq[:, qc, :], Oq[:, qc, :], alT[:, qc:qc + 1],
                                pvP[:], op0=ALU.mult, op1=ALU.add),
                            waits=[(sPE, tpv)], inc=sDV)

                prev_kt = None
                for i in range(len(pairs)):
                    kt, qc = pairs[i]
                    new_use = kt != prev_kt
                    if new_use:
                        if (qt, kt) != uses[n_use]:
                            n_use += 1
                        assert uses[n_use] == (qt, kt)
                        prev_kt = kt
                    if i > 0:
                        pair_mid(i - 1)
                    pair_front(i)
                    if i > 0:
                        pair_back(i - 1)
                    if new_use and n_use + 2 < len(uses):
                        load_kv(n_use + 2)
                pair_mid(len(pairs) - 1)
                pair_back(len(pairs) - 1)
                emit_lupdate(len(pairs) - 1)

                for qc in range(4):
                    q0 = qt * QT
                    p.emit("vector", lambda e, qc=qc: e.reciprocal(
                        t1T[:], lT[:, qc:qc + 1]))
                    vpad()
                    te = p.emit("vector", lambda e, qc=qc, qpar=qpar:
                        e.scalar_tensor_tensor(
                            x1t[qpar][:, qc, :], Oq[:, qc, :], t1T[:],
                            x1t[qpar][:, qc, :], op0=ALU.mult, op1=ALU.add),
                        waits=[x1tok[qpar]], inc=sDV)
                    x1read[qpar] = te
                    x2st[qpar] = p.dma("gpsimd",
                        lambda e, qc=qc, q0=q0, qpar=qpar: e.dma_start(
                            out=x2d[q0 + qc * 128:q0 + (qc + 1) * 128, :],
                            in_=x1t[qpar][:, qc, :]), waits=[(sDV, te)])
            p.barrier(ALL_SEMS)

        emit_ffn_pass(x2d, out_d, with_qkv=False, tag="c_")

        p.emit("gpsimd", lambda e: e.nop(),
               waits=[(s, s.n) for v in dma_sems.values() for s in v])

        block = E(nc.Block())
        p.replay(block)
    return nc



def _host_inputs(x_c, wq, wk, wv, w13, w2, w13s, w2s, gw, gb, S):
    perm = np.concatenate([np.arange(0, D, 2), np.arange(1, D, 2)])
    f32 = np.float32
    i = np.arange(D // 2, dtype=f32)
    theta = np.power(f32(10000.0), (f32(-2.0) * (i - f32(1.0)) / f32(D))).astype(f32)
    pos = np.arange(S, dtype=f32)
    ang = (pos[:, None] * theta[None, :]).astype(f32)
    cs = np.concatenate([np.cos(ang), np.sin(ang)], axis=1).astype(f32)

    d = {
        "wqT": np.ascontiguousarray(wq[perm].T),
        "wkT": np.ascontiguousarray(wk[perm].T),
        "wvT": np.ascontiguousarray(wv.T),
        "w13T": np.ascontiguousarray(np.transpose(w13, (0, 2, 1))),
        "w2T": np.ascontiguousarray(np.transpose(w2, (0, 2, 1))),
        "w13sT": np.ascontiguousarray(w13s.T),
        "w2sT": np.ascontiguousarray(w2s.T),
        "gwT": np.ascontiguousarray(gw.T),
        "gbb": np.ascontiguousarray(np.broadcast_to(gb, (128, NE))),
        "cs": cs,
    }
    d = {k: v.astype(np.float32) for k, v in d.items()}
    d["x"] = np.ascontiguousarray(x_c, dtype=np.float32)
    return d


def kernel(L0_attn_x, L0_attn_w_q, L0_attn_w_k, L0_attn_w_v,
           L0_ffn_input_ids, L0_ffn_gate_w, L0_ffn_gate_b,
           L0_ffn_w13, L0_ffn_w2, L0_ffn_w13_s, L0_ffn_w2_s):
    x = np.asarray(L0_attn_x, dtype=np.float32)
    B, S, _ = x.shape
    wq = np.asarray(L0_attn_w_q, dtype=np.float32)
    wk = np.asarray(L0_attn_w_k, dtype=np.float32)
    wv = np.asarray(L0_attn_w_v, dtype=np.float32)
    w13 = np.asarray(L0_ffn_w13, dtype=np.float32)
    w2 = np.asarray(L0_ffn_w2, dtype=np.float32)
    w13s = np.asarray(L0_ffn_w13_s, dtype=np.float32)
    w2s = np.asarray(L0_ffn_w2_s, dtype=np.float32)
    gw = np.asarray(L0_ffn_gate_w, dtype=np.float32)
    gb = np.asarray(L0_ffn_gate_b, dtype=np.float32)

    nc = build(S)
    shared = None
    in_maps = []
    for c in range(B):
        d = _host_inputs(x[c], wq, wk, wv, w13, w2, w13s, w2s, gw, gb, S)
        if shared is None:
            shared = {k: v for k, v in d.items() if k != "x"}
        else:
            for k in shared:
                d[k] = shared[k]
        in_maps.append(d)
    res = run_bass_kernel_spmd(nc, in_maps, core_ids=list(range(B)))
    out = np.stack([res.results[c]["out"] for c in range(B)], axis=0)
    return out
